# version 09
# baseline (speedup 1.0000x reference)
import contextlib
import numpy as np

import concourse.bass as bass
import concourse.mybir as mybir
from concourse.bass_utils import run_bass_kernel_spmd

F32 = mybir.dt.float32
BF16 = mybir.dt.bfloat16
AF = mybir.ActivationFunctionType
ALU = mybir.AluOpType
AX = mybir.AxisListType

B, S, D, H, DH = 2, 4096, 2048, 16, 128
DC = D // 128
E, HM, HS = 8, 1408, 2816
HMC, HSC = HM // 128, HS // 128
NP = 4
NT = 8
SCALE = float(1.0 / np.sqrt(128.0))
CSHIFT = 40.0
EPS = 1e-5
NCORES = 8

ENGS = ("sync", "gpsimd", "tensor", "vector", "scalar")


def _own_blocks(j):
    return [4 * i + ((j + i) % 4) for i in range(NP)]


def _panel_block_order(j, p):
    g = 4 * p + ((j + p) % 4)
    others = [blk for blk in range(4 * p, 4 * p + 4) if blk != g]
    return [g] + others


def _perm_tokens(j):
    order = []
    for p in range(NP):
        for blk in _panel_block_order(j, p):
            order.extend(range(256 * blk, 256 * blk + 256))
    return np.array(order, dtype=np.int64)


def _kv_block_order(p):
    return [4 * p + ((r + p) % 4) for r in range(4)]


def _masks(j):
    m = np.zeros((NP, 128, 2048), dtype=np.float32)
    for i in range(NP):
        g = 4 * i + ((j + i) % 4)
        order = _kv_block_order(i)
        qglob = 256 * g + np.arange(256)[None, :]
        kl = np.arange(128)[:, None]
        for rel in range(8):
            blk = order[rel // 2]
            kglob = 128 * (2 * blk + (rel % 2)) + kl
            m[i][:, 256 * rel:256 * rel + 256] = (kglob <= qglob).astype(np.float32)
    return m


class _Sem:
    __slots__ = ("h", "n", "name")

    def __init__(self, h, name):
        self.h = h
        self.n = 0
        self.name = name


class _Slot:

    __slots__ = ("sem", "val")

    def __init__(self):
        self.sem = None
        self.val = 0

    def set(self, sem, val=None):
        self.sem = sem
        self.val = sem.n if val is None else val


class _Sched:
    def __init__(self):
        self.q = {e: [] for e in ENGS}
        self.meta = {e: [] for e in ENGS}

    def emit(self, eng, fn, meta=("op",)):
        self.q[eng].append(fn)
        self.meta[eng].append(meta)

    def wait(self, eng, sem, val):
        assert val <= sem.n, f"forward wait on {sem.name}: {val} > {sem.n} (deadlock)"
        if val > 0:
            self.emit(eng, lambda Eg, s=sem, v=val: Eg.wait_ge(s.h, v), ("wait", sem, val))

    def wslot(self, eng, slot):
        if slot.sem is not None:
            self.wait(eng, slot.sem, slot.val)

    def check_deadlock(self):
        pos = {e: 0 for e in ENGS}
        cnt = {}
        progress = True
        while progress:
            progress = False
            for e in ENGS:
                while pos[e] < len(self.meta[e]):
                    m = self.meta[e][pos[e]]
                    if m[0] == "wait" and cnt.get(id(m[1]), 0) < m[2]:
                        break
                    if m[0] == "inc":
                        cnt[id(m[1])] = cnt.get(id(m[1]), 0) + m[2]
                    pos[e] += 1
                    progress = True
        stuck = {e: (pos[e], len(self.meta[e])) for e in ENGS if pos[e] < len(self.meta[e])}
        if stuck:
            detail = []
            for e, (p, n) in stuck.items():
                m = self.meta[e][p]
                detail.append(f"{e}@{p}/{n} waiting {m[1].name}>={m[2]} (have {cnt.get(id(m[1]), 0)})")
            raise RuntimeError("DEADLOCK: " + "; ".join(detail))


def build_program(cfg=None, taps=()):
    cfg = dict(cfg or {})
    rounds = cfg.get("rounds", NP)
    do_p2 = cfg.get("do_p2", True)
    do_o = cfg.get("do_o", True)
    do_ffn = cfg.get("do_ffn", True)
    experts = cfg.get("experts", 9)

    nc = bass.Bass("TRN2", target_bir_lowering=False)

    dx = nc.dram_tensor("x", [S, D], F32, kind="ExternalInput").ap()
    dwq = nc.dram_tensor("wq", [H, D, DH], F32, kind="ExternalInput").ap()
    dwk = nc.dram_tensor("wk", [H, D, DH], F32, kind="ExternalInput").ap()
    dwv = nc.dram_tensor("wv", [H, D, DH], F32, kind="ExternalInput").ap()
    dwo = nc.dram_tensor("wo", [H, DH, D], F32, kind="ExternalInput").ap()
    dbq = nc.dram_tensor("bq", [DH, H], F32, kind="ExternalInput").ap()
    dbk = nc.dram_tensor("bk", [DH, H], F32, kind="ExternalInput").ap()
    dbv = nc.dram_tensor("bv", [H, DH], F32, kind="ExternalInput").ap()
    dbo = nc.dram_tensor("bo", [D], F32, kind="ExternalInput").ap()
    dgw = nc.dram_tensor("gw", [D, E], F32, kind="ExternalInput").ap()
    dwg = nc.dram_tensor("wg", [E, D, HM], F32, kind="ExternalInput").ap()
    dwu = nc.dram_tensor("wu", [E, D, HM], F32, kind="ExternalInput").ap()
    dwd = nc.dram_tensor("wd", [E, HM, D], F32, kind="ExternalInput").ap()
    dshg = nc.dram_tensor("shg", [128, DC], F32, kind="ExternalInput").ap()
    dswg = nc.dram_tensor("swg", [D, HS], F32, kind="ExternalInput").ap()
    dswu = nc.dram_tensor("swu", [D, HS], F32, kind="ExternalInput").ap()
    dswd = nc.dram_tensor("swd", [HS, D], F32, kind="ExternalInput").ap()
    dmask = nc.dram_tensor("masks", [NP, 128, 2048], F32, kind="ExternalInput").ap()
    diden = nc.dram_tensor("iden", [128, 128], F32, kind="ExternalInput").ap()

    dout = nc.dram_tensor("out", [NP * 256, D], F32, kind="ExternalOutput").ap()

    dv = nc.dram_tensor("v_hbm", [S // 128, 128, H * DH], BF16).ap()
    dkta = nc.dram_tensor("kta_hbm", [NP, 4, H, 128, 256], BF16).ap()
    dktc = nc.dram_tensor("ktc_hbm", [H, 128, 256], BF16).ap()
    dvc = nc.dram_tensor("vc_hbm", [2, 128, H * DH], BF16).ap()
    dxo = nc.dram_tensor("xo_hbm", [NP * 256, D], F32).ap()
    dwkb = nc.dram_tensor("wk_bf_hbm", [H, 128, DC * 128], BF16).ap()
    dwqb = nc.dram_tensor("wq_bf_hbm", [H, 128, DC * 128], BF16).ap()
    dwvb = nc.dram_tensor("wv_bf_hbm", [2, 128, DC * 1024], BF16).ap()

    tap_aps = {}
    tshapes = {
        "tap_xf": ([128, DC * 1024], BF16), "tap_qt": ([128, H * 256], BF16),
        "tap_kt": ([H, 128, S], BF16), "tap_v": ([S // 128, 128, H * DH], BF16),
        "tap_zt": ([128, H * 1024], BF16), "tap_xown": ([NP * 256, D], F32),
        "tap_comb": ([128, 8 * 8], BF16), "tap_sg": ([128, 8], F32),
        "tap_xo": ([NP * 256, D], F32),
    }
    for t in taps:
        sh, dt_ = tshapes[t]
        tap_aps[t] = nc.dram_tensor(t, sh, dt_, kind="ExternalOutput").ap()

    off = 0

    def alloc(name, shape, dt, at=None):
        nonlocal off
        o = off if at is None else at
        t = nc.alloc_sbuf_tensor_at(name, list(shape), dt, offset=o)
        sz = shape[-1] * (2 if dt == BF16 else 4)
        if at is None:
            off = o + ((sz + 31) // 32) * 32
        return t.ap()

    KB = 1024
    off = 17 * KB

    iden = alloc("iden_b", [128, 128], BF16)
    one1 = alloc("one1", [1, 128], BF16)
    ones_col = alloc("ones_col", [128, 1], BF16)
    ones_sq = alloc("ones_sq", [128, 128], BF16)
    bq_sb = alloc("bq_sb", [128, H], F32)
    bk_sb = alloc("bk_sb", [128, H], F32)
    negc = alloc("negc", [128, H], F32)
    lns = alloc("lns", [128, 16], F32)
    ln2 = alloc("ln2", [128, 64], F32)
    sgc = alloc("sgc", [128, 8], F32)
    sgb = alloc("sgb", [128, 8], BF16)
    bvrow = alloc("bvrow", [1, 2048], BF16)
    borow = alloc("borow", [1, 2048], BF16)
    gwT = alloc("gwT", [128, DC * 8], BF16)
    shgT = alloc("shgT", [128, DC], BF16)
    rl_f = alloc("rl_f", [1, 256], F32)
    ones1f = alloc("ones1f", [1, 128], F32)
    eps_c = alloc("eps_c", [128, 1], F32)
    ncm = alloc("ncm", [1, 2], F32)
    ncb = alloc("ncb", [1, 2], BF16)
    comb = alloc("comb", [128, 8 * 8], BF16)
    cerow = alloc("cerow", [1, 1024], BF16)
    rlb_sb = alloc("rlb_sb", [128, 256], F32)
    ce_sb = alloc("ce_sb", [128, 1024], BF16)
    assert off <= 35 * KB, off

    xf = alloc("xf", [128, DC * 1024], BF16, at=35 * KB)
    zt = alloc("zt", [128, H * 1024], BF16, at=67 * KB)
    masks = alloc("masks", [128, NP * 2048], BF16, at=99 * KB)
    qt = alloc("qt", [128, H * 256], BF16, at=115 * KB)
    pt = [alloc(f"pt{i}", [128, 1024], BF16, at=123 * KB + 2048 * i) for i in range(2)]
    xln_tm = alloc("xln_tm", [128, 2048], BF16, at=127 * KB)
    x_own = [alloc(f"x_own{u}", [128, 2048], F32, at=67 * KB + u * 8 * KB) for u in range(8)]
    wv_bf = alloc("wv_bf", [128, DC * 1024], BF16, at=131 * KB)
    wk_bf = [alloc(f"wk_bf{i}", [128, DC * 128], BF16, at=163 * KB + i * 4 * KB) for i in range(2)]
    wq_bf = [alloc(f"wq_bf{i}", [128, DC * 128], BF16, at=171 * KB + i * 4 * KB) for i in range(2)]
    kt_blk = [alloc(f"kt_blk{i}", [128, S], BF16, at=131 * KB + i * 8 * KB) for i in range(2)]
    v_blk = [alloc(f"v_blk{i}", [128, 32 * 128], BF16, at=147 * KB + i * 8 * KB) for i in range(2)]
    wo_bf = alloc("wo_bf", [128, H * 2048], BF16, at=131 * KB)
    wd_bf = [alloc(f"wd_bf{i}", [128, 6 * 2048], BF16, at=131 * KB + i * 24 * KB) for i in range(2)]
    wgu_bf = [alloc(f"wgu_bf{i}", [128, 2 * DC * 128], BF16, at=179 * KB + i * 8 * KB) for i in range(2)]
    x_t = [alloc(f"x_t{i}", [128, 2048], F32, at=195 * KB + i * 8 * KB) for i in range(2)]
    kt_out = [alloc(f"kt_out{i}", [128, 512], BF16, at=211 * KB + i * KB) for i in range(2)]
    v_out = [alloc(f"v_out{i}", [128, 1024], BF16, at=213 * KB + i * 2 * KB) for i in range(2)]
    h_sb = alloc("h_sb", [128, HMC * 1024], BF16, at=195 * KB)
    sg_sb = alloc("sg_sb", [128, 1024], BF16, at=217 * KB)
    u_sb = alloc("u_sb", [128, 1024], BF16, at=219 * KB)
    assert 221 * KB <= 224 * KB - 1024

    ps = nc.alloc_psum_tensor("psall", [128, 4096], F32)
    ps_ap = ps.ap()

    def bank(i, n=1):
        return ps_ap[:, 512 * i:512 * i + 512 * n]

    ps_tr = bank(4).bitcast(BF16)

    st = contextlib.ExitStack()
    names = ["s_x", "s_w", "s_kw", "s_kr", "s_pe", "s_sc", "s_ve", "s_out", "s_gp", "s_cc"]
    SM = {n: _Sem(st.enter_context(nc.semaphore(n)), n) for n in names}
    s_x, s_w, s_kw, s_kr = SM["s_x"], SM["s_w"], SM["s_kw"], SM["s_kr"]
    s_pe, s_sc, s_ve, s_out = SM["s_pe"], SM["s_sc"], SM["s_ve"], SM["s_out"]
    s_gp = SM["s_gp"]
    s_cc = SM["s_cc"]

    sch = _Sched()

    def dma(eng, out, in_, sem, slow=False):
        sem.n += 16
        sch.emit(eng, lambda Eg, o=out, i=in_, s=sem, sl_=slow: Eg.dma_start(
            out=o, in_=i, allow_slow_non_contiguous=sl_).then_inc(s.h, 16),
            ("inc", sem, 16))
        return sem.n

    def run_inc(eng, sem, fn):
        sem.n += 1
        sch.emit(eng, lambda Eg, f=fn, s=sem: f(Eg).then_inc(s.h, 1), ("inc", sem, 1))
        return sem.n

    sl_x = [_Slot(), _Slot()]
    sl_ktout = [_Slot(), _Slot()]
    sl_vout = [_Slot(), _Slot()]
    sl_wk = [_Slot(), _Slot()]
    sl_wq = [_Slot(), _Slot()]
    sl_wv = _Slot()
    sl_kv = [_Slot(), _Slot()]
    sl_ss = [_Slot(), _Slot()]
    sl_pt = [_Slot(), _Slot()]
    sl_wgu = [_Slot(), _Slot()]
    sl_wd = [_Slot(), _Slot()]
    sl_qt = _Slot()
    sl_rlf = _Slot()
    sl_ncb = _Slot()
    sl_cerow = _Slot()
    sl_wk_w = [_Slot(), _Slot()]
    sl_wq_w = [_Slot(), _Slot()]
    sl_wv_w = _Slot()
    sl_xf = _Slot()
    sl_kt_psum = _Slot()
    sl_kt_psum2 = [_Slot(), _Slot()]
    sl_v_psum2 = [_Slot(), _Slot()]
    sl_q_psum = _Slot()
    sl_v_psum = _Slot()
    sl_tr_psum = _Slot()
    sl_zl_psum = _Slot()
    sl_rlb_psum = _Slot()

    dma("gpsimd", iden[:, :], diden[:, :], s_w)
    dma("gpsimd", masks.rearrange("p (i q) -> p i q", i=NP), dmask.transpose([1, 0, 2]), s_w)
    dma("gpsimd", bvrow[0:1, :], dbv.rearrange("h e -> (h e)").unsqueeze(0), s_w)
    dma("gpsimd", borow[0:1, :], dbo.unsqueeze(0), s_w)
    dma("gpsimd", gwT.rearrange("p (c e) -> p c e", c=DC), dgw.rearrange("(c p) e -> p c e", p=128), s_w)
    dma("gpsimd", shgT[:, :], dshg[:, :], s_w)
    p0_w = s_w.n
    dma("sync", bq_sb[:, :], dbq[:, :], s_x)
    dma("sync", bk_sb[:, :], dbk[:, :], s_x)
    p0_x = s_x.n

    sch.wait("vector", s_w, p0_w)
    sch.emit("vector", lambda Eg: Eg.memset(one1[0:1, :], 1.0))
    sch.emit("vector", lambda Eg: Eg.memset(ones1f[0:1, :], 1.0))
    sch.emit("vector", lambda Eg: Eg.memset(eps_c[:, 0:1], EPS))
    sch.emit("vector", lambda Eg: Eg.memset(ones_sq[:, :], 1.0))
    v_ones = run_inc("vector", s_ve, lambda Eg: Eg.memset(ones_col[:, 0:1], 1.0))

    xf3 = xf.rearrange("p (c t) -> p c t", c=DC)

    sl_xlntm = _Slot()
    ln_prev_apply = [0]

    def emit_ln(src_ap, gate_done_cb=None):
        sch.wait("vector", s_sc, ln_prev_apply[0])
        sch.emit("vector", lambda Eg, a=src_ap: Eg.reduce_sum(lns[:, 0:1], a, axis=AX.X))
        sch.wslot("vector", sl_xlntm)
        sch.emit("vector", lambda Eg, a=src_ap: Eg.scalar_tensor_tensor(
            xln_tm[:, :], a, 1.0, a, op0=ALU.mult, op1=ALU.mult, accum_out=lns[:, 1:2]))
        sch.emit("vector", lambda Eg: Eg.tensor_scalar(lns[:, 2:3], lns[:, 0:1], 1.0 / D, None, op0=ALU.mult))
        sch.emit("vector", lambda Eg: Eg.tensor_scalar(lns[:, 3:4], lns[:, 1:2], 1.0 / D, None, op0=ALU.mult))
        sch.emit("vector", lambda Eg: Eg.tensor_mul(lns[:, 4:5], lns[:, 2:3], lns[:, 2:3]))
        sch.emit("vector", lambda Eg: Eg.tensor_scalar(lns[:, 8:9], lns[:, 2:3], -1.0, None, op0=ALU.mult))
        v1 = run_inc("vector", s_ve, lambda Eg: Eg.scalar_tensor_tensor(
            lns[:, 5:6], lns[:, 4:5], -1.0, lns[:, 3:4], op0=ALU.mult, op1=ALU.add))
        sch.wait("scalar", s_ve, v1)
        lv = run_inc("scalar", s_sc, lambda Eg: Eg.activation(
            lns[:, 6:7], lns[:, 5:6], AF.Ln, bias=eps_c[:, 0:1], scale=1.0))
        sch.wait("scalar", s_sc, lv)
        rs = run_inc("scalar", s_sc, lambda Eg: Eg.activation(lns[:, 7:8], lns[:, 6:7], AF.Exp, bias=0.0, scale=-0.5))
        ln_prev_apply[0] = rs
        sch.wait("vector", s_sc, rs)
        r = run_inc("vector", s_ve, lambda Eg, a=src_ap: Eg.tensor_scalar(
            xln_tm[:, :], a, lns[:, 8:9], lns[:, 7:8], op0=ALU.add, op1=ALU.mult))
        return r

    def emit_transposes(col, s1, first_extra_wait=None):
        for bb in range(4):
            sch.wait("tensor", s_ve, s1)
            if first_extra_wait is not None and bb == 0:
                sem, val = first_extra_wait
                sch.wait("tensor", sem, val)
            sch.wslot("tensor", sl_tr_psum)

            def tr4(Eg, bb=bb):
                for c4 in range(4):
                    c = 4 * bb + c4
                    i = Eg.transpose(ps_tr[:, 128 * c4:128 * c4 + 128],
                                     xln_tm[:, 128 * c:128 * c + 128], iden[:, :])
                return i
            tv = run_inc("tensor", s_pe, tr4)
            sch.wait("vector", s_pe, tv)
            sch.wslot("vector", sl_xf)
            run_inc("vector", s_ve, lambda Eg, bb=bb, col=col: Eg.tensor_copy(
                xf3[:, 4 * bb:4 * bb + 4, col:col + 128],
                ps_tr[:, 0:512].rearrange("p (c t) -> p c t", c=4)))
            sl_tr_psum.set(s_ve)
        sl_xlntm.set(s_pe)

    pe_after_p1 = [0] * NP
    pe_after_p2 = [0] * NP
    sc_after_p2 = [0] * NP
    ve_xf = [0] * NP
    kvw_after_p1 = [0] * NP
    qt_last_evac = [0] * NP
    oproj_ve = {}
    negc_ready = 0
    NG = [2 * (pp + 1) for pp in range(NP)]

    for p in range(rounds):
        if p > 0:
            sl_xf.set(s_pe, pe_after_p1[p - 1])
            sl_tr_psum.set(s_ve, sl_tr_psum.val if sl_tr_psum.sem is s_ve else s_ve.n)
        for tt in range(2):
            sl = tt % 2
            sch.wslot("sync", sl_x[sl])
            xrow = 1024 * p + 128 * tt
            dma("sync", x_t[sl][:, :], dx[xrow:xrow + 128, :], s_x)
            xv = s_x.n
            sch.wait("vector", s_x, xv)
            s1 = emit_ln(x_t[sl][:, :])
            sl_x[sl].set(s_ve, s1)
            few = None
            if p > 0 and tt == 0:
                few = (s_ve, ve_zt_all_read)
            elif p == 0 and tt == 0:
                few = (s_w, p0_w)
            emit_transposes(128 * tt, s1, few)
        ve_xf[p] = s_ve.n
        sl_xf.set(s_ve, ve_xf[p])

        for h in range(H):
            sl = h % 2
            sch.wslot("gpsimd", sl_wk[sl])
            if h == 0 and p > 0:
                sch.wait("gpsimd", s_pe, pe_after_p2[p - 1])
            if p == 0:
                dma("gpsimd", wk_bf[sl].rearrange("p (c e) -> p c e", c=DC),
                    dwk[h].rearrange("(c p) e -> p c e", p=128), s_w)
                sch.wslot("gpsimd", sl_wq[sl])
                dma("gpsimd", wq_bf[sl].rearrange("p (c e) -> p c e", c=DC),
                    dwq[h].rearrange("(c p) e -> p c e", p=128), s_w)
                wv_ = s_w.n
                sch.wslot("gpsimd", sl_wk_w[sl])
                sch.wslot("gpsimd", sl_wq_w[sl])
                sch.wait("sync", s_w, wv_)
                dma("sync", dwkb[h][:, :], wk_bf[sl][:, :], s_kw)
                sl_wk_w[sl].set(s_kw)
                dma("sync", dwqb[h][:, :], wq_bf[sl][:, :], s_kw)
                sl_wq_w[sl].set(s_kw)
            else:
                sch.wslot("gpsimd", sl_wk_w[sl])
                if h == 0 and p == 1:
                    sch.wait("gpsimd", s_kw, kvw_after_p1[0])
                dma("gpsimd", wk_bf[sl][:, :], dwkb[h][:, :], s_w)
                sch.wslot("gpsimd", sl_wq[sl])
                sch.wslot("gpsimd", sl_wq_w[sl])
                dma("gpsimd", wq_bf[sl][:, :], dwqb[h][:, :], s_w)
                wv_ = s_w.n
            sch.wait("tensor", s_w, wv_)
            sch.wait("tensor", s_ve, ve_xf[p])
            if p > 0 and h == 0:
                sch.wait("tensor", s_sc, sc_after_p2[p - 1])
            if h == 0:
                sch.wait("sync", s_cc, 2 * p)
            kb = 0 if h % 2 == 0 else 3
            sch.wslot("tensor", sl_kt_psum2[h % 2])

            def ktmm(Eg, sl=sl, kb=kb):
                for c in range(DC):
                    i = Eg.matmul(bank(kb)[:, 0:256], wk_bf[sl][:, 128 * c:128 * c + 128],
                                  xf3[:, c, 0:256], start=(c == 0), stop=(c == DC - 1))
                return i
            kv_ = run_inc("tensor", s_pe, ktmm)
            ko = h % 2
            sch.wait("scalar", s_pe, kv_)
            sch.wslot("scalar", sl_ktout[ko])
            if p == 0 and h == 0:
                sch.wait("scalar", s_x, p0_x)
            ev = run_inc("scalar", s_sc, lambda Eg, ko=ko, h=h, kb=kb: Eg.activation(
                kt_out[ko][:, 0:256], bank(kb)[:, 0:256], AF.Identity,
                bias=bk_sb[:, h:h + 1], scale=1.0))
            sl_kt_psum2[h % 2].set(s_sc, ev)
            sch.wait("sync", s_sc, ev)
            dma("sync", dktc[h][:, :], kt_out[ko][:, 0:256], s_kw)
            sl_ktout[ko].set(s_kw)
            sl_wk[sl].set(s_pe)
            sch.wslot("tensor", sl_q_psum)
            if p > 0 and h == 0:
                sch.wait("tensor", s_pe, pe_after_p2[p - 1])

            def qmm(Eg, sl=sl):
                for c in range(DC):
                    i = Eg.matmul(bank(7)[:, 0:256], wq_bf[sl][:, 128 * c:128 * c + 128],
                                  xf3[:, c, 0:256], start=(c == 0), stop=(c == DC - 1))
                return i
            qv_ = run_inc("tensor", s_pe, qmm)
            sl_wq[sl].set(s_pe)
            sch.wait("scalar", s_pe, qv_)
            qe = run_inc("scalar", s_sc, lambda Eg, h=h: Eg.activation(
                qt[:, 256 * h:256 * h + 256], bank(7)[:, 0:256], AF.Identity,
                bias=bq_sb[:, h:h + 1], scale=1.0))
            sl_q_psum.set(s_sc, qe)
        qt_last_evac[p] = s_sc.n

        wv4 = wv_bf.rearrange("p (c n) -> p c n", c=DC)
        for hh in range(2):
            sch.wslot("gpsimd", sl_wv)
            sch.wslot("gpsimd", sl_wv_w)
            if hh == 0 and p > 0:
                sch.wait("gpsimd", s_pe, pe_after_p2[p - 1])
            if p == 0:
                wv5 = wv_bf.rearrange("p (c h e) -> p c h e", c=DC, h=8)
                for h8 in range(8):
                    dma("gpsimd", wv5[:, :, h8, :],
                        dwv[8 * hh + h8].rearrange("(c p) e -> p c e", p=128), s_w)
                wv_ = s_w.n
                sch.wait("sync", s_w, wv_)
                dma("sync", dwvb[hh][:, :], wv_bf[:, :], s_kw)
                sl_wv_w.set(s_kw)
            else:
                dma("gpsimd", wv_bf[:, :], dwvb[hh][:, :], s_w)
                wv_ = s_w.n
            sch.wait("tensor", s_w, wv_)
            sch.wait("tensor", s_ve, ve_xf[p])
            sch.wait("tensor", s_ve, v_ones)
            for tt in range(2):
                vb = 1 if tt % 2 == 0 else 5
                sch.wslot("tensor", sl_v_psum2[tt % 2])
                if p > 0 and hh == 0 and tt == 1:
                    sch.wait("tensor", s_ve, ve_zt_all_read)

                def vmm(Eg, tt=tt, hh=hh, vb=vb):
                    for c in range(DC):
                        for q in range(2):
                            Eg.matmul(bank(vb, 2)[:, 512 * q:512 * q + 512],
                                      xf3[:, c, 128 * tt:128 * tt + 128],
                                      wv4[:, c, 512 * q:512 * q + 512],
                                      start=(c == 0), stop=False, skip_group_check=True)
                    for q in range(2):
                        i = Eg.matmul(bank(vb, 2)[:, 512 * q:512 * q + 512], one1[0:1, :],
                                      bvrow[0:1, 1024 * hh + 512 * q:1024 * hh + 512 * q + 512],
                                      start=False, stop=True, skip_group_check=True)
                    return i
                vv_ = run_inc("tensor", s_pe, vmm)
                vo = tt % 2
                sch.wait("vector", s_pe, vv_)
                sch.wslot("vector", sl_vout[vo])
                vev_ = run_inc("vector", s_ve, lambda Eg, vo=vo, vb=vb: Eg.tensor_copy(
                    v_out[vo][:, :], bank(vb, 2)))
                sl_v_psum2[tt % 2].set(s_ve, vev_)
                sch.wait("sync", s_ve, vev_)
                dma("sync", dvc[tt][:, 1024 * hh:1024 * hh + 1024], v_out[vo][:, :], s_kw)
                sl_vout[vo].set(s_kw)
            sl_wv.set(s_pe)
        pe_after_p1[p] = s_pe.n
        kvw_after_p1[p] = s_kw.n
        sch.wait("gpsimd", s_kw, kvw_after_p1[p])
        s_cc.n += 1
        sch.emit("gpsimd", lambda Eg, p=p: Eg.collective_compute(
            "AllGather", ALU.bypass, replica_groups=[[0, 1, 2, 3], [4, 5, 6, 7]],
            ins=[dktc[:, :, :]], outs=[dkta[p]],
        ).then_inc(s_cc.h, 1), ("inc", s_cc, 1))
        s_cc.n += 1
        sch.emit("gpsimd", lambda Eg, p=p: Eg.collective_compute(
            "AllGather", ALU.bypass, replica_groups=[[0, 1, 2, 3], [4, 5, 6, 7]],
            ins=[dvc[:, :, :]],
            outs=[dv[8 * p:8 * p + 8].rearrange("(r c) p n -> r c p n", r=4)],
        ).then_inc(s_cc.h, 1), ("inc", s_cc, 1))
        cc_round = s_cc.n

        if not do_p2:
            ve_zt_all_read = s_ve.n
            pe_after_p2[p] = s_pe.n
            sc_after_p2[p] = s_sc.n
            continue
        if p == 0:
            for h in range(H):
                sch.wait("tensor", s_x, p0_x)
                sch.wslot("tensor", sl_zl_psum)
                nv = run_inc("tensor", s_pe, lambda Eg, h=h: Eg.matmul(
                    bank(5)[0:1, 0:1], bq_sb[:, h:h + 1], bk_sb[:, h:h + 1],
                    start=True, stop=True))
                sch.wait("vector", s_pe, nv)
                nm_ = run_inc("vector", s_ve, lambda Eg: Eg.tensor_scalar(
                    ncm[0:1, 0:1], bank(5)[0:1, 0:1], -SCALE, -CSHIFT, op0=ALU.mult, op1=ALU.add))
                sch.wait("vector", s_ve, nm_)
                sch.wslot("vector", sl_ncb)
                nb = run_inc("vector", s_ve, lambda Eg: Eg.tensor_copy(ncb[0:1, 0:1], ncm[0:1, 0:1]))
                sl_zl_psum.set(s_ve, nb)
                sch.wait("tensor", s_ve, nb)
                sch.wslot("tensor", sl_rlb_psum)
                bv2 = run_inc("tensor", s_pe, lambda Eg: Eg.matmul(
                    bank(6)[:, 0:1], one1[0:1, :], ncb[0:1, 0:1], start=True, stop=True))
                sl_ncb.set(s_pe, bv2)
                sch.wait("vector", s_pe, bv2)
                nr = run_inc("vector", s_ve, lambda Eg, h=h: Eg.tensor_copy(
                    negc[:, h:h + 1], bank(6)[:, 0:1]))
                sl_rlb_psum.set(s_ve, nr)
            negc_ready = s_ve.n

        nchunks = 8 * (p + 1)
        mask3 = masks.rearrange("p (i q) -> p i q", i=NP)
        for h in range(H):
            sl = h % 2
            sch.wait("sync", s_cc, cc_round)
            sch.wslot("sync", sl_kv[sl])
            if h == 0:
                sch.wait("sync", s_pe, pe_after_p1[p])
            for i in range(p + 1):
                dma("sync", kt_blk[sl][:, 1024 * i:1024 * i + 1024].rearrange("p (r k) -> p r k", r=4),
                    dkta[i][:, h, :, :].transpose([1, 0, 2]), s_kr)
            dma("sync", v_blk[sl][:, 0:128 * nchunks].rearrange("p (s e) -> p s e", e=128),
                dv[0:nchunks, :, DH * h:DH * h + DH].transpose([1, 0, 2]), s_kr)
            kv_r = s_kr.n
            sch.wait("tensor", s_kr, kv_r)
            if h == 0:
                sch.wait("tensor", s_sc, qt_last_evac[p])
                sch.wait("tensor", s_ve, negc_ready)
                sch.wslot("tensor", sl_kt_psum2[0])
                sch.wslot("tensor", sl_kt_psum2[1])
                sch.wslot("tensor", sl_v_psum2[0])
                sch.wslot("tensor", sl_v_psum2[1])
            for grp in range(NG[p]):
                sb = grp % 2
                sch.wslot("tensor", sl_ss[sb])

                def smm(Eg, sb=sb, grp=grp, sl=sl, h=h):
                    for c4 in range(4):
                        ch = 4 * grp + c4
                        i = Eg.matmul(bank(2 * sb, 2)[:, 256 * c4:256 * c4 + 256],
                                      kt_blk[sl][:, 128 * ch:128 * ch + 128],
                                      qt[:, 256 * h:256 * h + 256],
                                      start=True, stop=True, skip_group_check=True)
                    return i
                sv_ = run_inc("tensor", s_pe, smm)
                sch.wait("scalar", s_pe, sv_)
                sch.wslot("scalar", sl_pt[grp % 2])
                ex = run_inc("scalar", s_sc, lambda Eg, sb=sb, grp=grp, h=h: Eg.activation(
                    pt[grp % 2][:, :], bank(2 * sb, 2), AF.Exp,
                    bias=negc[:, h:h + 1], scale=SCALE))
                sl_ss[sb].set(s_sc, ex)
                masked = grp >= NG[p] - 2
                if masked:
                    rel = grp - (NG[p] - 2)
                    sch.wait("gpsimd", s_sc, ex)
                    mk = run_inc("gpsimd", s_gp, lambda Eg, grp=grp, rel=rel, p=p: Eg.tensor_mul(
                        pt[grp % 2][:, :], pt[grp % 2][:, :],
                        mask3[:, p, 1024 * rel:1024 * rel + 1024]))
                    sch.wait("tensor", s_gp, mk)
                else:
                    sch.wait("tensor", s_sc, ex)
                if grp == 0:
                    sch.wslot("tensor", sl_zl_psum)

                def avmm(Eg, grp=grp, sl=sl, nchunks=nchunks):
                    for c4 in range(4):
                        ch = 4 * grp + c4
                        first = (ch == 0)
                        last = (ch == nchunks - 1)
                        Eg.matmul(bank(4)[:, 0:256],
                                  v_blk[sl][:, 128 * ch:128 * ch + 128],
                                  pt[grp % 2][:, 256 * c4:256 * c4 + 256],
                                  start=first, stop=last, skip_group_check=True)
                        i = Eg.matmul(bank(5)[:, 0:256], ones_sq[:, :],
                                      pt[grp % 2][:, 256 * c4:256 * c4 + 256],
                                      start=first, stop=last, skip_group_check=True)
                    return i
                av_ = run_inc("tensor", s_pe, avmm)
                sl_pt[grp % 2].set(s_pe, av_)
            sl_kv[sl].set(s_pe)
            sch.wait("vector", s_pe, s_pe.n)
            rb = run_inc("vector", s_ve, lambda Eg: Eg.reciprocal(rlb_sb[:, :], bank(5)[:, 0:256]))
            sch.wait("vector", s_ve, rb)
            ze = run_inc("vector", s_ve, lambda Eg, h=h, p=p: Eg.tensor_tensor(
                zt[:, 1024 * h + 256 * p:1024 * h + 256 * p + 256],
                bank(4)[:, 0:256], rlb_sb[:, :], op=ALU.mult))
            sl_zl_psum.set(s_ve, ze)
        pe_after_p2[p] = s_pe.n
        sc_after_p2[p] = s_sc.n
        ve_zt_all_read = s_ve.n

    ln2_done = {}
    pe_ln2 = 0
    kw_xo = 0
    if do_o and do_p2 and rounds == NP:
        sch.wait("gpsimd", s_pe, pe_after_p2[NP - 1])
        for h in range(H):
            dma("gpsimd", wo_bf[:, 2048 * h:2048 * h + 2048], dwo[h][:, :], s_w)
        wo_w = s_w.n
        ve_all_z = s_ve.n
        sl_x_kw = [_Slot(), _Slot()]
        sl_xf.set(s_pe, pe_after_p1[NP - 1])
        for u in range(8):
            pp, kk = u // 2, u % 2
            sl = u % 2
            sch.wslot("sync", sl_x[sl])
            sch.wslot("sync", sl_x_kw[sl])
            xrow = 1024 * pp + 128 * kk
            dma("sync", x_t[sl][:, :], dx[xrow:xrow + 128, :], s_x)
            xv = s_x.n
            if u == 0:
                sch.wait("tensor", s_w, wo_w)
                sch.wait("tensor", s_ve, ve_all_z)
                sch.wait("tensor", s_sc, sc_after_p2[NP - 1])
                sch.wslot("tensor", sl_kt_psum2[0])
                sch.wslot("tensor", sl_kt_psum2[1])
                sch.wslot("tensor", sl_v_psum2[0])
            if u >= 1:
                sch.wait("tensor", s_ve, oproj_ve[u - 1])

            def omm_half(Eg, pp=pp, kk=kk, qs=(0, 1)):
                for h in range(H):
                    zsl = zt[:, 1024 * h + 256 * pp + 128 * kk:1024 * h + 256 * pp + 128 * kk + 128]
                    for q in qs:
                        Eg.matmul(bank(0, 4)[:, 512 * q:512 * q + 512], zsl,
                                  wo_bf[:, 2048 * h + 512 * q:2048 * h + 512 * q + 512],
                                  start=(h == 0), stop=False, skip_group_check=True)
                for q in qs:
                    i = Eg.matmul(bank(0, 4)[:, 512 * q:512 * q + 512], one1[0:1, :],
                                  borow[0:1, 512 * q:512 * q + 512],
                                  start=False, stop=True, skip_group_check=True)
                return i
            ov_lo = run_inc("tensor", s_pe, lambda Eg, f=omm_half: f(Eg, qs=(0, 1)))
            ov_hi = run_inc("tensor", s_pe, lambda Eg, f=omm_half: f(Eg, qs=(2, 3)))
            sch.wait("vector", s_pe, ov_lo)
            sch.wait("vector", s_x, xv)
            oe_lo = run_inc("vector", s_ve, lambda Eg, sl=sl: Eg.tensor_tensor(
                x_t[sl][:, 0:1024], bank(0, 2), x_t[sl][:, 0:1024], op=ALU.add))
            sch.wait("vector", s_pe, ov_hi)
            oe = run_inc("vector", s_ve, lambda Eg, sl=sl: Eg.tensor_tensor(
                x_t[sl][:, 1024:2048], bank(2, 2), x_t[sl][:, 1024:2048], op=ALU.add))
            oproj_ve[u] = oe
            sch.wait("sync", s_ve, oe)
            dma("sync", dxo[128 * u:128 * u + 128, :], x_t[sl][:, :], s_kw)
            sl_x_kw[sl].set(s_kw)
            sch.wait("vector", s_ve, oe)
            s1 = emit_ln(x_t[sl][:, :])
            sl_x[sl].set(s_ve, s1)
            few = (s_ve, ve_zt_all_read) if u == 0 else None
            emit_transposes(128 * u, s1, few)
            ln2_done[u] = s_ve.n
        pe_ln2 = s_pe.n
        kw_xo = s_kw.n

    if do_ffn and do_o and do_p2 and rounds == NP:
        ve_xf2 = s_ve.n

        comb_ve = {}
        rt_prev_sc = [0]
        gw3 = gwT.rearrange("p (c e) -> p c e", c=DC)
        rt_gate = [None]
        for u in range(8):
            sch.wait("tensor", s_ve, ln2_done[u])
            if u == 0:
                sch.wait("tensor", s_w, p0_w)
                sch.wslot("tensor", sl_zl_psum)
                sch.wslot("tensor", sl_rlb_psum)
            else:
                sch.wait("tensor", s_sc, rt_gate[0][0])
                sch.wait("tensor", s_ve, rt_gate[0][1])

            def rmm(Eg, u=u):
                for c in range(DC):
                    Eg.matmul(bank(5)[:, 0:8], xf3[:, c, 128 * u:128 * u + 128],
                              gw3[:, c, :], start=(c == 0), stop=(c == DC - 1),
                              skip_group_check=True)
                    i = Eg.matmul(bank(6)[:, 0:1], xf3[:, c, 128 * u:128 * u + 128],
                                  shgT[:, c:c + 1], start=(c == 0), stop=(c == DC - 1),
                                  skip_group_check=True)
                return i
            rv = run_inc("tensor", s_pe, rmm)
            sch.wait("scalar", s_pe, rv)
            sgv = run_inc("scalar", s_sc, lambda Eg, u=u: Eg.activation(
                sgc[:, u:u + 1], bank(6)[:, 0:1], AF.Sigmoid))
            sgv_gate = sgv
            sch.wait("vector", s_pe, rv)
            sch.wait("vector", s_sc, rt_prev_sc[0])

            def VS(fn):
                v = run_inc("vector", s_ve, fn)
                sch.wait("vector", s_ve, v)
                return v
            cp5 = VS(lambda Eg: Eg.tensor_copy(ln2[:, 0:8], bank(5)[:, 0:8]))
            rt_gate[0] = (sgv_gate, cp5)
            VS(lambda Eg: Eg.reduce_max(ln2[:, 8:9], ln2[:, 0:8], axis=AX.X))
            VS(lambda Eg: Eg.tensor_scalar(ln2[:, 9:10], ln2[:, 8:9], -1.0, None, op0=ALU.mult))
            VS(lambda Eg: Eg.tensor_scalar(ln2[:, 10:18], ln2[:, 0:8], ln2[:, 8:9], None, op0=ALU.is_ge))
            VS(lambda Eg: Eg.tensor_scalar(ln2[:, 18:26], ln2[:, 10:18], -1e9, None, op0=ALU.mult))
            VS(lambda Eg: Eg.tensor_add(ln2[:, 26:34], ln2[:, 18:26], ln2[:, 0:8]))
            VS(lambda Eg: Eg.reduce_max(ln2[:, 34:35], ln2[:, 26:34], axis=AX.X))
            VS(lambda Eg: Eg.tensor_scalar(ln2[:, 35:43], ln2[:, 26:34], ln2[:, 34:35], None, op0=ALU.is_ge))
            kv_ = VS(lambda Eg: Eg.tensor_add(ln2[:, 43:51], ln2[:, 35:43], ln2[:, 10:18]))
            sch.wait("scalar", s_ve, kv_)
            pe_ = run_inc("scalar", s_sc, lambda Eg: Eg.activation(
                ln2[:, 51:59], ln2[:, 0:8], AF.Exp, bias=ln2[:, 9:10], scale=1.0))
            rt_prev_sc[0] = pe_
            sch.wait("vector", s_sc, pe_)
            VS(lambda Eg: Eg.tensor_mul(ln2[:, 18:26], ln2[:, 51:59], ln2[:, 43:51]))
            VS(lambda Eg: Eg.reduce_sum(ln2[:, 59:60], ln2[:, 18:26], axis=AX.X))
            VS(lambda Eg: Eg.reciprocal(ln2[:, 60:61], ln2[:, 59:60]))
            cb = VS(lambda Eg, u=u: Eg.tensor_scalar(
                comb[:, 8 * u:8 * u + 8], ln2[:, 18:26], ln2[:, 60:61], None, op0=ALU.mult))
            comb_ve[u] = cb
            if u == 0:
                sl_zl_psum.set(s_ve, cb)
                sl_rlb_psum.set(s_sc, sgv)
        sch.wait("vector", s_sc, s_sc.n)
        sgv2 = run_inc("vector", s_ve, lambda Eg: Eg.tensor_copy(sgb[:, :], sgc[:, :]))
        comb_ready = sgv2

        xo_r = {}
        sch.wait("sync", s_pe, pe_ln2)
        sch.wait("sync", s_ve, s_ve.n)
        sch.wait("sync", s_kw, kw_xo)
        for u in range(8):
            dma("sync", x_own[u][:, :], dxo[128 * u:128 * u + 128, :], s_x)
            xo_r[u] = s_x.n

        dn_ev = {}
        dn_lo_prev = [0]
        dn_hi_prev = [0]
        ce_done = {}
        h_done = {}
        wgu_ctr = [0]
        wd_ctr = [0]
        first_ffn = [True]

        def phaseA(ee):
            shared = (ee == 8)
            nhc = HSC if shared else HMC
            halves = 2 if shared else 1
            for hf in range(halves):
                if hf == 0:
                    sch.wait("sync", s_ve, comb_ready)
                    sch.wslot("sync", sl_cerow)
                    for uu in range(8):
                        csrc = comb[:, 8 * uu + ee:8 * uu + ee + 1] if ee < 8 else sgb[:, uu:uu + 1]
                        dma("sync", cerow[0:1, 128 * uu:128 * uu + 128], csrc, s_x)
                    cw = s_x.n
                    sch.wait("tensor", s_x, cw)
                    if ee > 0:
                        sch.wait("tensor", s_ve, dn_ev.get((ee - 1, 7), 0) or 0)
                    if first_ffn[0]:
                        sch.wait("tensor", s_ve, oproj_ve[7])
                        first_ffn[0] = False
                    c1 = run_inc("tensor", s_pe, lambda Eg: Eg.matmul(
                        bank(0)[:, 0:512], one1[0:1, :],
                        cerow[0:1, 0:512], start=True, stop=True))
                    c2 = run_inc("tensor", s_pe, lambda Eg: Eg.matmul(
                        bank(1)[:, 0:512], one1[0:1, :],
                        cerow[0:1, 512:1024], start=True, stop=True))
                    sl_cerow.set(s_pe, c2)
                    sch.wait("vector", s_pe, c2)
                    sch.emit("vector", lambda Eg: Eg.tensor_copy(ce_sb[:, 0:512], bank(0)[:, 0:512]))
                    cev = run_inc("vector", s_ve, lambda Eg: Eg.tensor_copy(
                        ce_sb[:, 512:1024], bank(1)[:, 0:512]))
                    ce_done[ee] = cev
                nh = nhc // halves if shared else nhc
                for ih in range(nh):
                    hc = hf * HMC + ih
                    slw = wgu_ctr[0] % 2
                    wgu_ctr[0] += 1
                    if ee == 0 and hf == 0 and ih == 0:
                        sch.wait("gpsimd", s_pe, pe_ln2)
                    sch.wslot("gpsimd", sl_wgu[slw])
                    if shared:
                        srcg = dswg[:, 128 * hc:128 * hc + 128].rearrange("(c p) m -> p c m", p=128)
                        srcu = dswu[:, 128 * hc:128 * hc + 128].rearrange("(c p) m -> p c m", p=128)
                    else:
                        srcg = dwg[ee][:, 128 * hc:128 * hc + 128].rearrange("(c p) m -> p c m", p=128)
                        srcu = dwu[ee][:, 128 * hc:128 * hc + 128].rearrange("(c p) m -> p c m", p=128)
                    dma("gpsimd", wgu_bf[slw][:, 0:DC * 128].rearrange("p (c m) -> p c m", c=DC), srcg, s_w)
                    dma("gpsimd", wgu_bf[slw][:, DC * 128:2 * DC * 128].rearrange("p (c m) -> p c m", c=DC), srcu, s_w)
                    ww = s_w.n
                    sch.wait("tensor", s_w, ww)
                    if ee == 0 and ih == 0:
                        sch.wait("tensor", s_ve, ve_xf2)
                    g2 = h_done.get((ee, hf, ih - 1))
                    if g2:
                        sch.wait("tensor", s_ve, g2)
                    elif hf == 1 and ih == 0:
                        sch.wait("tensor", s_ve, h_done[(ee, 0, nh - 1)])
                    if ih == 0 and hf == 0:
                        sch.wait("tensor", s_ve, ce_done[ee])

                    def gumm(Eg, slw=slw):
                        w3 = wgu_bf[slw].rearrange("p (g c m) -> p g c m", g=2, c=DC)
                        for c in range(DC):
                            for q in range(2):
                                Eg.matmul(bank(0, 2)[:, 512 * q:512 * q + 512], w3[:, 0, c, :],
                                          xf3[:, c, 512 * q:512 * q + 512],
                                          start=(c == 0), stop=(c == DC - 1), skip_group_check=True)
                                i = Eg.matmul(bank(2, 2)[:, 512 * q:512 * q + 512], w3[:, 1, c, :],
                                              xf3[:, c, 512 * q:512 * q + 512],
                                              start=(c == 0), stop=(c == DC - 1), skip_group_check=True)
                        return i
                    gv = run_inc("tensor", s_pe, gumm)
                    sl_wgu[slw].set(s_pe, gv)
                    sch.wait("scalar", s_pe, gv)
                    g3 = h_done.get((ee, hf, ih - 1)) or (h_done.get((ee, 0, nh - 1)) if hf else None)
                    if g3:
                        sch.wait("scalar", s_ve, g3)
                    sgv_ = run_inc("scalar", s_sc, lambda Eg: Eg.activation(
                        sg_sb[:, :], bank(0, 2)[:, 0:1024], AF.Silu))
                    sch.wait("vector", s_pe, gv)
                    sch.wait("vector", s_ve, ce_done[ee])
                    uv_ = run_inc("vector", s_ve, lambda Eg: Eg.tensor_tensor(
                        u_sb[:, :], bank(2, 2)[:, 0:1024], ce_sb[:, :], op=ALU.mult))
                    sch.wait("vector", s_ve, uv_)
                    sch.wait("vector", s_sc, sgv_)
                    if ee == 0 and hf == 0 and ih == 0:
                        sch.wait("vector", s_kw, s_kw.n)
                        sch.wait("vector", s_pe, pe_ln2)
                    hv = run_inc("vector", s_ve, lambda Eg, ih=ih: Eg.tensor_mul(
                        h_sb[:, 1024 * ih:1024 * ih + 1024], sg_sb[:, :], u_sb[:, :]))
                    h_done[(ee, hf, ih)] = hv
                yield hf

        def phaseB(ee, hf):
            shared = (ee == 8)
            nhc = HSC if shared else HMC
            halves = 2 if shared else 1
            nh = nhc // halves if shared else nhc
            pieces = [(0, list(range(0, min(6, nh)))), (1, list(range(6, nh)))]
            for slw, chs in pieces:
                if not chs:
                    continue
                if ee == 0 and hf == 0:
                    sch.wait("gpsimd", s_pe, pe_ln2)
                sch.wslot("gpsimd", sl_wd[slw])
                for k, ihc in enumerate(chs):
                    hc = hf * HMC + ihc
                    srcd = dswd[128 * hc:128 * hc + 128, :] if shared else dwd[ee][128 * hc:128 * hc + 128, :]
                    dma("gpsimd", wd_bf[slw][:, 2048 * k:2048 * k + 2048], srcd, s_w)
            ww = s_w.n
            sch.wait("tensor", s_w, ww)
            sch.wait("tensor", s_ve, h_done[(ee, hf, nh - 1)])
            for u in range(8):
                if dn_lo_prev[0]:
                    sch.wait("tensor", s_ve, dn_lo_prev[0])
                elif ee == 0 and u == 0:
                    sch.wait("tensor", s_ve, s_ve.n)
                    sch.wait("tensor", s_sc, s_sc.n)

                def dmm_half(Eg, u=u, nh=nh, qs=(0, 1)):
                    for ihc in range(nh):
                        slw, k = (0, ihc) if ihc < 6 else (1, ihc - 6)
                        for q in qs:
                            i = Eg.matmul(bank(4, 4)[:, 512 * q:512 * q + 512],
                                          h_sb[:, 1024 * ihc + 128 * u:1024 * ihc + 128 * u + 128],
                                          wd_bf[slw][:, 2048 * k + 512 * q:2048 * k + 512 * q + 512],
                                          start=(ihc == 0), stop=(ihc == nh - 1),
                                          skip_group_check=True)
                    return i
                dv_lo = run_inc("tensor", s_pe, lambda Eg, f=dmm_half: f(Eg, qs=(0, 1)))
                if dn_hi_prev[0]:
                    sch.wait("tensor", s_ve, dn_hi_prev[0])
                dv_ = run_inc("tensor", s_pe, lambda Eg, f=dmm_half: f(Eg, qs=(2, 3)))
                sch.wait("vector", s_pe, dv_lo)
                if ee == 0 and hf == 0:
                    sch.wait("vector", s_x, xo_r[u])
                de_lo = run_inc("vector", s_ve, lambda Eg, u=u: Eg.tensor_tensor(
                    x_own[u][:, 0:1024], x_own[u][:, 0:1024], bank(4, 2), op=ALU.add))
                dn_lo_prev[0] = de_lo
                sch.wait("vector", s_pe, dv_)
                de = run_inc("vector", s_ve, lambda Eg, u=u: Eg.tensor_tensor(
                    x_own[u][:, 1024:2048], x_own[u][:, 1024:2048], bank(6, 2), op=ALU.add))
                dn_hi_prev[0] = de
                dn_ev[(ee, u if hf == 0 else 8 + u)] = de
            for slw, chs in pieces:
                if chs:
                    sl_wd[slw].set(s_pe)

        for ee in range(experts):
            for hf in phaseA(ee):
                phaseB(ee, hf)

        last = experts - 1
        for u in range(8):
            key = (last, 8 + u) if last == 8 else (last, u)
            sch.wait("sync", s_ve, dn_ev[key])
            dma("sync", dout[128 * u:128 * u + 128, :], x_own[u][:, :], s_out)
        sch.wait("sync", s_out, s_out.n)

    if taps:
        tsrc = {"tap_xf": xf, "tap_qt": qt, "tap_zt": zt, "tap_comb": comb, "tap_sg": sgc}
        for t in taps:
            if t == "tap_v":
                sch.wait("sync", s_kw, s_kw.n)
                dma("sync", tap_aps[t][:, :, :], dv[:, :, :], s_out)
            elif t == "tap_xo":
                sch.wait("sync", s_kw, s_kw.n)
                dma("sync", tap_aps[t][:, :], dxo[:, :], s_out)
            elif t == "tap_xown":
                sch.wait("sync", s_ve, s_ve.n)
                for u in range(8):
                    dma("sync", tap_aps[t][128 * u:128 * u + 128, :], x_own[u][:, :], s_out)
            else:
                sch.wait("sync", s_ve, s_ve.n)
                sch.wait("sync", s_sc, s_sc.n)
                dma("sync", tap_aps[t][:, :], tsrc[t][:, :], s_out)
        sch.wait("sync", s_out, s_out.n)

    sch.check_deadlock()
    with nc.Block() as block:
        @block.sync
        def _(Eg):
            for f in sch.q["sync"]:
                f(Eg)

        @block.gpsimd
        def _(Eg):
            for f in sch.q["gpsimd"]:
                f(Eg)

        @block.tensor
        def _(Eg):
            for f in sch.q["tensor"]:
                f(Eg)

        @block.vector
        def _(Eg):
            for f in sch.q["vector"]:
                f(Eg)

        @block.scalar
        def _(Eg):
            for f in sch.q["scalar"]:
                f(Eg)

    st.close()
    return nc


def _host_inputs(inputs, core):
    b, j = core // 4, core % 4
    perm = _perm_tokens(j)
    x = np.ascontiguousarray(np.asarray(inputs["L0_attn_x"])[b][perm])
    return {
        "x": x,
        "wq": np.asarray(inputs["L0_attn_W_Q"]),
        "wk": np.asarray(inputs["L0_attn_W_K"]),
        "wv": np.asarray(inputs["L0_attn_W_V"]),
        "wo": np.asarray(inputs["L0_attn_W_O"]),
        "bq": np.ascontiguousarray(np.asarray(inputs["L0_attn_b_Q"]).T),
        "bk": np.ascontiguousarray(np.asarray(inputs["L0_attn_b_K"]).T),
        "bv": np.asarray(inputs["L0_attn_b_V"]),
        "bo": np.asarray(inputs["L0_attn_b_O"]),
        "gw": np.ascontiguousarray(np.asarray(inputs["L0_ffn_gate_w"]).T),
        "wg": np.asarray(inputs["L0_ffn_w_gate"]),
        "wu": np.asarray(inputs["L0_ffn_w_up"]),
        "wd": np.asarray(inputs["L0_ffn_w_down"]),
        "shg": np.ascontiguousarray(np.asarray(inputs["L0_ffn_shgate_w"]).reshape(DC, 128).T),
        "swg": np.asarray(inputs["L0_ffn_shw_gate"]),
        "swu": np.asarray(inputs["L0_ffn_shw_up"]),
        "swd": np.asarray(inputs["L0_ffn_shw_down"]),
        "masks": _masks(j),
        "iden": np.eye(128, dtype=np.float32),
    }


def kernel(**inputs):
    nc = build_program()
    in_maps = [_host_inputs(inputs, c) for c in range(NCORES)]
    res = run_bass_kernel_spmd(nc, in_maps, core_ids=list(range(NCORES)))
    out = np.zeros((B, S, D), dtype=np.float32)
    for c in range(NCORES):
        b, j = c // 4, c % 4
        o = res.results[c]["out"]
        for i, blk in enumerate(_own_blocks(j)):
            out[b, 256 * blk:256 * blk + 256, :] = o[256 * i:256 * i + 256, :]
    return out


# version 10
# speedup vs baseline: 1.1091x; 1.1091x over previous
import contextlib
import numpy as np

import concourse.bass as bass
import concourse.mybir as mybir
from concourse.bass_utils import run_bass_kernel_spmd

F32 = mybir.dt.float32
BF16 = mybir.dt.bfloat16
AF = mybir.ActivationFunctionType
ALU = mybir.AluOpType
AX = mybir.AxisListType

B, S, D, H, DH = 2, 4096, 2048, 16, 128
DC = D // 128
E, HM, HS = 8, 1408, 2816
HMC, HSC = HM // 128, HS // 128
NP = 4
NT = 8
SCALE = float(1.0 / np.sqrt(128.0))
CSHIFT = 40.0
EPS = 1e-5
NCORES = 8

ENGS = ("sync", "gpsimd", "tensor", "vector", "scalar")


def _own_blocks(j):
    return [4 * i + ((j + i) % 4) for i in range(NP)]


def _panel_block_order(j, p):
    g = 4 * p + ((j + p) % 4)
    others = [blk for blk in range(4 * p, 4 * p + 4) if blk != g]
    return [g] + others


def _perm_tokens(j):
    order = []
    for p in range(NP):
        for blk in _panel_block_order(j, p):
            order.extend(range(256 * blk, 256 * blk + 256))
    return np.array(order, dtype=np.int64)


def _kv_block_order(p):
    return [4 * p + ((r + p) % 4) for r in range(4)]


def _masks(j):
    m = np.zeros((NP, 128, 2048), dtype=np.float32)
    for i in range(NP):
        g = 4 * i + ((j + i) % 4)
        order = _kv_block_order(i)
        qglob = 256 * g + np.arange(256)[None, :]
        kl = np.arange(128)[:, None]
        for rel in range(8):
            blk = order[rel // 2]
            kglob = 128 * (2 * blk + (rel % 2)) + kl
            m[i][:, 256 * rel:256 * rel + 256] = (kglob <= qglob).astype(np.float32)
    return m


class _Sem:
    __slots__ = ("h", "n", "name")

    def __init__(self, h, name):
        self.h = h
        self.n = 0
        self.name = name


class _Slot:

    __slots__ = ("sem", "val")

    def __init__(self):
        self.sem = None
        self.val = 0

    def set(self, sem, val=None):
        self.sem = sem
        self.val = sem.n if val is None else val


class _Sched:
    def __init__(self):
        self.q = {e: [] for e in ENGS}
        self.meta = {e: [] for e in ENGS}

    def emit(self, eng, fn, meta=("op",)):
        self.q[eng].append(fn)
        self.meta[eng].append(meta)

    def wait(self, eng, sem, val):
        assert val <= sem.n, f"forward wait on {sem.name}: {val} > {sem.n} (deadlock)"
        if val > 0:
            self.emit(eng, lambda Eg, s=sem, v=val: Eg.wait_ge(s.h, v), ("wait", sem, val))

    def wslot(self, eng, slot):
        if slot.sem is not None:
            self.wait(eng, slot.sem, slot.val)

    def check_deadlock(self):
        pos = {e: 0 for e in ENGS}
        cnt = {}
        progress = True
        while progress:
            progress = False
            for e in ENGS:
                while pos[e] < len(self.meta[e]):
                    m = self.meta[e][pos[e]]
                    if m[0] == "wait" and cnt.get(id(m[1]), 0) < m[2]:
                        break
                    if m[0] == "inc":
                        cnt[id(m[1])] = cnt.get(id(m[1]), 0) + m[2]
                    pos[e] += 1
                    progress = True
        stuck = {e: (pos[e], len(self.meta[e])) for e in ENGS if pos[e] < len(self.meta[e])}
        if stuck:
            detail = []
            for e, (p, n) in stuck.items():
                m = self.meta[e][p]
                detail.append(f"{e}@{p}/{n} waiting {m[1].name}>={m[2]} (have {cnt.get(id(m[1]), 0)})")
            raise RuntimeError("DEADLOCK: " + "; ".join(detail))


def build_program(cfg=None, taps=()):
    cfg = dict(cfg or {})
    rounds = cfg.get("rounds", NP)
    do_p2 = cfg.get("do_p2", True)
    do_o = cfg.get("do_o", True)
    do_ffn = cfg.get("do_ffn", True)
    experts = cfg.get("experts", 9)

    nc = bass.Bass("TRN2", target_bir_lowering=False)

    dx = nc.dram_tensor("x", [S, D], F32, kind="ExternalInput").ap()
    dwq = nc.dram_tensor("wq", [H, D, DH], F32, kind="ExternalInput").ap()
    dwk = nc.dram_tensor("wk", [H, D, DH], F32, kind="ExternalInput").ap()
    dwv = nc.dram_tensor("wv", [H, D, DH], F32, kind="ExternalInput").ap()
    dwo = nc.dram_tensor("wo", [H, DH, D], F32, kind="ExternalInput").ap()
    dbq = nc.dram_tensor("bq", [DH, H], F32, kind="ExternalInput").ap()
    dbk = nc.dram_tensor("bk", [DH, H], F32, kind="ExternalInput").ap()
    dbv = nc.dram_tensor("bv", [H, DH], F32, kind="ExternalInput").ap()
    dbo = nc.dram_tensor("bo", [D], F32, kind="ExternalInput").ap()
    dgw = nc.dram_tensor("gw", [D, E], F32, kind="ExternalInput").ap()
    dwg = nc.dram_tensor("wg", [E, D, HM], F32, kind="ExternalInput").ap()
    dwu = nc.dram_tensor("wu", [E, D, HM], F32, kind="ExternalInput").ap()
    dwd = nc.dram_tensor("wd", [E, HM, D], F32, kind="ExternalInput").ap()
    dshg = nc.dram_tensor("shg", [128, DC], F32, kind="ExternalInput").ap()
    dswg = nc.dram_tensor("swg", [D, HS], F32, kind="ExternalInput").ap()
    dswu = nc.dram_tensor("swu", [D, HS], F32, kind="ExternalInput").ap()
    dswd = nc.dram_tensor("swd", [HS, D], F32, kind="ExternalInput").ap()
    dmask = nc.dram_tensor("masks", [NP, 128, 2048], F32, kind="ExternalInput").ap()
    diden = nc.dram_tensor("iden", [128, 128], F32, kind="ExternalInput").ap()

    dout = nc.dram_tensor("out", [NP * 256, D], F32, kind="ExternalOutput").ap()

    dv = nc.dram_tensor("v_hbm", [S // 128, 128, H * DH], BF16).ap()
    dkta = nc.dram_tensor("kta_hbm", [NP, 4, H, 128, 256], BF16).ap()
    dktc2 = [nc.dram_tensor(f"ktc_hbm{i}", [H, 128, 256], BF16).ap() for i in range(2)]
    dvc2 = [nc.dram_tensor(f"vc_hbm{i}", [2, 128, H * DH], BF16).ap() for i in range(2)]
    dxo = nc.dram_tensor("xo_hbm", [NP * 256, D], F32).ap()
    dwkb = nc.dram_tensor("wk_bf_hbm", [H, 128, DC * 128], BF16).ap()
    dwqb = nc.dram_tensor("wq_bf_hbm", [H, 128, DC * 128], BF16).ap()
    dwvb = nc.dram_tensor("wv_bf_hbm", [2, 128, DC * 1024], BF16).ap()

    tap_aps = {}
    tshapes = {
        "tap_xf": ([128, DC * 1024], BF16), "tap_qt": ([128, H * 256], BF16),
        "tap_kt": ([H, 128, S], BF16), "tap_v": ([S // 128, 128, H * DH], BF16),
        "tap_zt": ([128, H * 1024], BF16), "tap_xown": ([NP * 256, D], F32),
        "tap_comb": ([128, 8 * 8], BF16), "tap_sg": ([128, 8], F32),
        "tap_xo": ([NP * 256, D], F32),
    }
    for t in taps:
        sh, dt_ = tshapes[t]
        tap_aps[t] = nc.dram_tensor(t, sh, dt_, kind="ExternalOutput").ap()

    off = 0

    def alloc(name, shape, dt, at=None):
        nonlocal off
        o = off if at is None else at
        t = nc.alloc_sbuf_tensor_at(name, list(shape), dt, offset=o)
        sz = shape[-1] * (2 if dt == BF16 else 4)
        if at is None:
            off = o + ((sz + 31) // 32) * 32
        return t.ap()

    KB = 1024
    off = 17 * KB

    iden = alloc("iden_b", [128, 128], BF16)
    one1 = alloc("one1", [1, 128], BF16)
    ones_col = alloc("ones_col", [128, 1], BF16)
    ones_sq = alloc("ones_sq", [128, 128], BF16)
    bq_sb = alloc("bq_sb", [128, H], F32)
    bk_sb = alloc("bk_sb", [128, H], F32)
    negc = alloc("negc", [128, H], F32)
    lns = alloc("lns", [128, 16], F32)
    ln2 = alloc("ln2", [128, 64], F32)
    sgc = alloc("sgc", [128, 8], F32)
    sgb = alloc("sgb", [128, 8], BF16)
    bvrow = alloc("bvrow", [1, 2048], BF16)
    borow = alloc("borow", [1, 2048], BF16)
    gwT = alloc("gwT", [128, DC * 8], BF16)
    shgT = alloc("shgT", [128, DC], BF16)
    rl_f = alloc("rl_f", [1, 256], F32)
    ones1f = alloc("ones1f", [1, 128], F32)
    eps_c = alloc("eps_c", [128, 1], F32)
    ncm = alloc("ncm", [1, 2], F32)
    ncb = alloc("ncb", [1, 2], BF16)
    comb = alloc("comb", [128, 8 * 8], BF16)
    cerow = alloc("cerow", [1, 1024], BF16)
    rlb_sb = alloc("rlb_sb", [128, 256], F32)
    ce_sb = alloc("ce_sb", [128, 1024], BF16)
    assert off <= 35 * KB, off

    xf = alloc("xf", [128, DC * 1024], BF16, at=35 * KB)
    zt = alloc("zt", [128, H * 1024], BF16, at=67 * KB)
    masks2 = [alloc(f"masks{i}", [128, 2048], BF16, at=99 * KB + i * 4 * KB) for i in range(2)]
    qt2 = [alloc(f"qt{i}", [128, H * 256], BF16, at=107 * KB + i * 8 * KB) for i in range(2)]
    pt = [alloc(f"pt{i}", [128, 1024], BF16, at=123 * KB + 2048 * i) for i in range(2)]
    xln_tm = alloc("xln_tm", [128, 2048], BF16, at=127 * KB)
    x_own = [alloc(f"x_own{u}", [128, 2048], F32, at=67 * KB + u * 8 * KB) for u in range(8)]
    wv_bf = alloc("wv_bf", [128, DC * 1024], BF16, at=131 * KB)
    wk_bf = [alloc(f"wk_bf{i}", [128, DC * 128], BF16, at=163 * KB + i * 4 * KB) for i in range(2)]
    wq_bf = [alloc(f"wq_bf{i}", [128, DC * 128], BF16, at=171 * KB + i * 4 * KB) for i in range(2)]
    kt_blk = [alloc(f"kt_blk{i}", [128, S], BF16, at=131 * KB + i * 8 * KB) for i in range(2)]
    v_blk = [alloc(f"v_blk{i}", [128, 32 * 128], BF16, at=147 * KB + i * 8 * KB) for i in range(2)]
    wo_bf = alloc("wo_bf", [128, H * 2048], BF16, at=131 * KB)
    wd_bf = [alloc(f"wd_bf{i}", [128, 6 * 2048], BF16, at=131 * KB + i * 24 * KB) for i in range(2)]
    wgu_bf = [alloc(f"wgu_bf{i}", [128, 2 * DC * 128], BF16, at=179 * KB + i * 8 * KB) for i in range(2)]
    x_t = [alloc(f"x_t{i}", [128, 2048], F32, at=195 * KB + i * 8 * KB) for i in range(2)]
    kt_out = [alloc(f"kt_out{i}", [128, 512], BF16, at=211 * KB + i * KB) for i in range(2)]
    v_out = [alloc(f"v_out{i}", [128, 1024], BF16, at=213 * KB + i * 2 * KB) for i in range(2)]
    h_sb = alloc("h_sb", [128, HMC * 1024], BF16, at=195 * KB)
    sg_sb = alloc("sg_sb", [128, 1024], BF16, at=217 * KB)
    u_sb = alloc("u_sb", [128, 1024], BF16, at=219 * KB)
    assert 221 * KB <= 224 * KB - 1024

    ps = nc.alloc_psum_tensor("psall", [128, 4096], F32)
    ps_ap = ps.ap()

    def bank(i, n=1):
        return ps_ap[:, 512 * i:512 * i + 512 * n]

    ps_tr = bank(4).bitcast(BF16)

    st = contextlib.ExitStack()
    names = ["s_x", "s_w", "s_kw", "s_kr", "s_pe", "s_sc", "s_ve", "s_out", "s_gp", "s_cc"]
    SM = {n: _Sem(st.enter_context(nc.semaphore(n)), n) for n in names}
    s_x, s_w, s_kw, s_kr = SM["s_x"], SM["s_w"], SM["s_kw"], SM["s_kr"]
    s_pe, s_sc, s_ve, s_out = SM["s_pe"], SM["s_sc"], SM["s_ve"], SM["s_out"]
    s_gp = SM["s_gp"]
    s_cc = SM["s_cc"]

    sch = _Sched()

    def dma(eng, out, in_, sem, slow=False):
        sem.n += 16
        sch.emit(eng, lambda Eg, o=out, i=in_, s=sem, sl_=slow: Eg.dma_start(
            out=o, in_=i, allow_slow_non_contiguous=sl_).then_inc(s.h, 16),
            ("inc", sem, 16))
        return sem.n

    def run_inc(eng, sem, fn):
        sem.n += 1
        sch.emit(eng, lambda Eg, f=fn, s=sem: f(Eg).then_inc(s.h, 1), ("inc", sem, 1))
        return sem.n

    sl_x = [_Slot(), _Slot()]
    sl_ktout = [_Slot(), _Slot()]
    sl_vout = [_Slot(), _Slot()]
    sl_wk = [_Slot(), _Slot()]
    sl_wq = [_Slot(), _Slot()]
    sl_wv = _Slot()
    sl_kv = [_Slot(), _Slot()]
    sl_ss = [_Slot(), _Slot()]
    sl_pt = [_Slot(), _Slot()]
    sl_wgu = [_Slot(), _Slot()]
    sl_wd = [_Slot(), _Slot()]
    sl_qt = _Slot()
    sl_rlf = _Slot()
    sl_ncb = _Slot()
    sl_cerow = _Slot()
    sl_wk_w = [_Slot(), _Slot()]
    sl_wq_w = [_Slot(), _Slot()]
    sl_wv_w = _Slot()
    sl_masks = [_Slot(), _Slot()]
    sl_qt2 = [_Slot(), _Slot()]
    sl_xf = _Slot()
    sl_kt_psum = _Slot()
    sl_kt_psum2 = [_Slot(), _Slot()]
    sl_v_psum2 = [_Slot(), _Slot()]
    sl_q_psum = _Slot()
    sl_v_psum = _Slot()
    sl_tr_psum = _Slot()
    sl_zl_psum = _Slot()
    sl_rlb_psum = _Slot()

    dma("gpsimd", iden[:, :], diden[:, :], s_w)
    dma("gpsimd", bvrow[0:1, :], dbv.rearrange("h e -> (h e)").unsqueeze(0), s_w)
    dma("gpsimd", borow[0:1, :], dbo.unsqueeze(0), s_w)
    dma("gpsimd", gwT.rearrange("p (c e) -> p c e", c=DC), dgw.rearrange("(c p) e -> p c e", p=128), s_w)
    dma("gpsimd", shgT[:, :], dshg[:, :], s_w)
    p0_w = s_w.n
    dma("sync", bq_sb[:, :], dbq[:, :], s_x)
    dma("sync", bk_sb[:, :], dbk[:, :], s_x)
    p0_x = s_x.n

    sch.wait("vector", s_w, p0_w)
    sch.emit("vector", lambda Eg: Eg.memset(one1[0:1, :], 1.0))
    sch.emit("vector", lambda Eg: Eg.memset(ones1f[0:1, :], 1.0))
    sch.emit("vector", lambda Eg: Eg.memset(eps_c[:, 0:1], EPS))
    sch.emit("vector", lambda Eg: Eg.memset(ones_sq[:, :], 1.0))
    v_ones = run_inc("vector", s_ve, lambda Eg: Eg.memset(ones_col[:, 0:1], 1.0))

    xf3 = xf.rearrange("p (c t) -> p c t", c=DC)

    sl_xlntm = _Slot()
    ln_prev_apply = [0]

    def emit_ln(src_ap, gate_done_cb=None):
        sch.wait("vector", s_sc, ln_prev_apply[0])
        sch.emit("vector", lambda Eg, a=src_ap: Eg.reduce_sum(lns[:, 0:1], a, axis=AX.X))
        sch.wslot("vector", sl_xlntm)
        sch.emit("vector", lambda Eg, a=src_ap: Eg.scalar_tensor_tensor(
            xln_tm[:, :], a, 1.0, a, op0=ALU.mult, op1=ALU.mult, accum_out=lns[:, 1:2]))
        sch.emit("vector", lambda Eg: Eg.tensor_scalar(lns[:, 2:3], lns[:, 0:1], 1.0 / D, None, op0=ALU.mult))
        sch.emit("vector", lambda Eg: Eg.tensor_scalar(lns[:, 3:4], lns[:, 1:2], 1.0 / D, None, op0=ALU.mult))
        sch.emit("vector", lambda Eg: Eg.tensor_mul(lns[:, 4:5], lns[:, 2:3], lns[:, 2:3]))
        sch.emit("vector", lambda Eg: Eg.tensor_scalar(lns[:, 8:9], lns[:, 2:3], -1.0, None, op0=ALU.mult))
        v1 = run_inc("vector", s_ve, lambda Eg: Eg.scalar_tensor_tensor(
            lns[:, 5:6], lns[:, 4:5], -1.0, lns[:, 3:4], op0=ALU.mult, op1=ALU.add))
        sch.wait("scalar", s_ve, v1)
        lv = run_inc("scalar", s_sc, lambda Eg: Eg.activation(
            lns[:, 6:7], lns[:, 5:6], AF.Ln, bias=eps_c[:, 0:1], scale=1.0))
        sch.wait("scalar", s_sc, lv)
        rs = run_inc("scalar", s_sc, lambda Eg: Eg.activation(lns[:, 7:8], lns[:, 6:7], AF.Exp, bias=0.0, scale=-0.5))
        ln_prev_apply[0] = rs
        sch.wait("vector", s_sc, rs)
        r = run_inc("vector", s_ve, lambda Eg, a=src_ap: Eg.tensor_scalar(
            xln_tm[:, :], a, lns[:, 8:9], lns[:, 7:8], op0=ALU.add, op1=ALU.mult))
        return r

    def emit_transposes(col, s1, first_extra_wait=None):
        for bb in range(4):
            sch.wait("tensor", s_ve, s1)
            if first_extra_wait is not None and bb == 0:
                sem, val = first_extra_wait
                sch.wait("tensor", sem, val)
            sch.wslot("tensor", sl_tr_psum)

            def tr4(Eg, bb=bb):
                for c4 in range(4):
                    c = 4 * bb + c4
                    i = Eg.transpose(ps_tr[:, 128 * c4:128 * c4 + 128],
                                     xln_tm[:, 128 * c:128 * c + 128], iden[:, :])
                return i
            tv = run_inc("tensor", s_pe, tr4)
            sch.wait("vector", s_pe, tv)
            sch.wslot("vector", sl_xf)
            run_inc("vector", s_ve, lambda Eg, bb=bb, col=col: Eg.tensor_copy(
                xf3[:, 4 * bb:4 * bb + 4, col:col + 128],
                ps_tr[:, 0:512].rearrange("p (c t) -> p c t", c=4)))
            sl_tr_psum.set(s_ve)
        sl_xlntm.set(s_pe)

    pe_after_p1 = [0] * NP
    pe_after_p2 = [0] * NP
    sc_after_p2 = [0] * NP
    ve_xf = [0] * NP
    kvw_after_p1 = [0] * NP
    qt_last_evac = [0] * NP
    oproj_ve = {}
    negc_ready = 0
    NG = [2 * (pp + 1) for pp in range(NP)]

    st_p2 = {"pe": 0, "sc": 0, "ve_zt": 0}
    negc_state = [0]

    def build_round(p):
        nonlocal_vars = None
        sch.wslot("gpsimd", sl_masks[p % 2])
        dma("gpsimd", masks2[p % 2][:, :], dmask[p], s_w)
        if p > 0:
            sl_xf.set(s_pe, pe_after_p1[p - 1])
        for tt in range(2):
            sl = tt % 2
            sch.wslot("sync", sl_x[sl])
            xrow = 1024 * p + 128 * tt
            dma("sync", x_t[sl][:, :], dx[xrow:xrow + 128, :], s_x)
            xv = s_x.n
            sch.wait("vector", s_x, xv)
            s1 = emit_ln(x_t[sl][:, :])
            sl_x[sl].set(s_ve, s1)
            few = None
            if tt == 0 and p == 0:
                few = (s_w, p0_w)
            elif tt == 0:
                few = (s_ve, st_p2["ve_zt"])
            emit_transposes(128 * tt, s1, few)
        ve_xf[p] = s_ve.n
        sl_xf.set(s_ve, ve_xf[p])

        for h in range(H):
            sl = h % 2
            sch.wslot("gpsimd", sl_wk[sl])
            if h == 0 and p > 0:
                sch.wait("gpsimd", s_pe, st_p2["pe"])
            if p == 0:
                dma("gpsimd", wk_bf[sl].rearrange("p (c e) -> p c e", c=DC),
                    dwk[h].rearrange("(c p) e -> p c e", p=128), s_w)
                sch.wslot("gpsimd", sl_wq[sl])
                dma("gpsimd", wq_bf[sl].rearrange("p (c e) -> p c e", c=DC),
                    dwq[h].rearrange("(c p) e -> p c e", p=128), s_w)
                wv_ = s_w.n
                sch.wslot("gpsimd", sl_wk_w[sl])
                sch.wslot("gpsimd", sl_wq_w[sl])
                sch.wait("sync", s_w, wv_)
                dma("sync", dwkb[h][:, :], wk_bf[sl][:, :], s_kw)
                sl_wk_w[sl].set(s_kw)
                dma("sync", dwqb[h][:, :], wq_bf[sl][:, :], s_kw)
                sl_wq_w[sl].set(s_kw)
            else:
                sch.wslot("gpsimd", sl_wk_w[sl])
                if h == 0 and p == 1:
                    sch.wait("gpsimd", s_kw, kvw_after_p1[0])
                dma("gpsimd", wk_bf[sl][:, :], dwkb[h][:, :], s_w)
                sch.wslot("gpsimd", sl_wq[sl])
                sch.wslot("gpsimd", sl_wq_w[sl])
                dma("gpsimd", wq_bf[sl][:, :], dwqb[h][:, :], s_w)
                wv_ = s_w.n
            sch.wait("tensor", s_w, wv_)
            sch.wait("tensor", s_ve, ve_xf[p])
            if h == 0:
                sch.wait("tensor", s_sc, st_p2["sc"])
                sch.wait("sync", s_cc, max(0, 2 * (p - 1)))
            kb = 0 if h % 2 == 0 else 3
            sch.wslot("tensor", sl_kt_psum2[h % 2])

            def ktmm(Eg, sl=sl, kb=kb):
                for c in range(DC):
                    i = Eg.matmul(bank(kb)[:, 0:256], wk_bf[sl][:, 128 * c:128 * c + 128],
                                  xf3[:, c, 0:256], start=(c == 0), stop=(c == DC - 1))
                return i
            kv_ = run_inc("tensor", s_pe, ktmm)
            ko = h % 2
            sch.wait("scalar", s_pe, kv_)
            sch.wslot("scalar", sl_ktout[ko])
            if p == 0 and h == 0:
                sch.wait("scalar", s_x, p0_x)
            ev = run_inc("scalar", s_sc, lambda Eg, ko=ko, h=h, kb=kb: Eg.activation(
                kt_out[ko][:, 0:256], bank(kb)[:, 0:256], AF.Identity,
                bias=bk_sb[:, h:h + 1], scale=1.0))
            sl_kt_psum2[h % 2].set(s_sc, ev)
            sch.wait("sync", s_sc, ev)
            dma("sync", dktc2[p % 2][h][:, :], kt_out[ko][:, 0:256], s_kw)
            sl_ktout[ko].set(s_kw)
            sl_wk[sl].set(s_pe)
            sch.wslot("tensor", sl_q_psum)
            if h == 0:
                sch.wslot("tensor", sl_qt2[p % 2])

            def qmm(Eg, sl=sl):
                for c in range(DC):
                    i = Eg.matmul(bank(7)[:, 0:256], wq_bf[sl][:, 128 * c:128 * c + 128],
                                  xf3[:, c, 0:256], start=(c == 0), stop=(c == DC - 1))
                return i
            qv_ = run_inc("tensor", s_pe, qmm)
            sl_wq[sl].set(s_pe)
            sch.wait("scalar", s_pe, qv_)
            qe = run_inc("scalar", s_sc, lambda Eg, h=h, p=p: Eg.activation(
                qt2[p % 2][:, 256 * h:256 * h + 256], bank(7)[:, 0:256], AF.Identity,
                bias=bq_sb[:, h:h + 1], scale=1.0))
            sl_q_psum.set(s_sc, qe)
        qt_last_evac[p] = s_sc.n

        wv4 = wv_bf.rearrange("p (c n) -> p c n", c=DC)
        for hh in range(2):
            sch.wslot("gpsimd", sl_wv)
            sch.wslot("gpsimd", sl_wv_w)
            if hh == 0 and p > 0:
                sch.wait("gpsimd", s_pe, st_p2["pe"])
            if p == 0:
                wv5 = wv_bf.rearrange("p (c h e) -> p c h e", c=DC, h=8)
                for h8 in range(8):
                    dma("gpsimd", wv5[:, :, h8, :],
                        dwv[8 * hh + h8].rearrange("(c p) e -> p c e", p=128), s_w)
                wv_ = s_w.n
                sch.wait("sync", s_w, wv_)
                dma("sync", dwvb[hh][:, :], wv_bf[:, :], s_kw)
                sl_wv_w.set(s_kw)
            else:
                dma("gpsimd", wv_bf[:, :], dwvb[hh][:, :], s_w)
                wv_ = s_w.n
            sch.wait("tensor", s_w, wv_)
            sch.wait("tensor", s_ve, ve_xf[p])
            sch.wait("tensor", s_ve, v_ones)
            for tt in range(2):
                vb = 1 if tt % 2 == 0 else 5
                sch.wslot("tensor", sl_v_psum2[tt % 2])
                if hh == 0 and tt == 1:
                    sch.wait("tensor", s_ve, st_p2["ve_zt"])

                def vmm(Eg, tt=tt, hh=hh, vb=vb):
                    for c in range(DC):
                        for q in range(2):
                            Eg.matmul(bank(vb, 2)[:, 512 * q:512 * q + 512],
                                      xf3[:, c, 128 * tt:128 * tt + 128],
                                      wv4[:, c, 512 * q:512 * q + 512],
                                      start=(c == 0), stop=False, skip_group_check=True)
                    for q in range(2):
                        i = Eg.matmul(bank(vb, 2)[:, 512 * q:512 * q + 512], one1[0:1, :],
                                      bvrow[0:1, 1024 * hh + 512 * q:1024 * hh + 512 * q + 512],
                                      start=False, stop=True, skip_group_check=True)
                    return i
                vv_ = run_inc("tensor", s_pe, vmm)
                vo = tt % 2
                sch.wait("vector", s_pe, vv_)
                sch.wslot("vector", sl_vout[vo])
                vev_ = run_inc("vector", s_ve, lambda Eg, vo=vo, vb=vb: Eg.tensor_copy(
                    v_out[vo][:, :], bank(vb, 2)))
                sl_v_psum2[tt % 2].set(s_ve, vev_)
                sch.wait("sync", s_ve, vev_)
                dma("sync", dvc2[p % 2][tt][:, 1024 * hh:1024 * hh + 1024], v_out[vo][:, :], s_kw)
                sl_vout[vo].set(s_kw)
            sl_wv.set(s_pe)
        pe_after_p1[p] = s_pe.n
        kvw_after_p1[p] = s_kw.n
        sch.wait("gpsimd", s_kw, kvw_after_p1[p])
        s_cc.n += 1
        sch.emit("gpsimd", lambda Eg, p=p: Eg.collective_compute(
            "AllGather", ALU.bypass, replica_groups=[[0, 1, 2, 3], [4, 5, 6, 7]],
            ins=[dktc2[p % 2][:, :, :]], outs=[dkta[p]],
        ).then_inc(s_cc.h, 1), ("inc", s_cc, 1))
        s_cc.n += 1
        sch.emit("gpsimd", lambda Eg, p=p: Eg.collective_compute(
            "AllGather", ALU.bypass, replica_groups=[[0, 1, 2, 3], [4, 5, 6, 7]],
            ins=[dvc2[p % 2][:, :, :]],
            outs=[dv[8 * p:8 * p + 8].rearrange("(r c) p n -> r c p n", r=4)],
        ).then_inc(s_cc.h, 1), ("inc", s_cc, 1))

    def attn_round(p):
        if p == 0:
            for h in range(H):
                sch.wait("tensor", s_x, p0_x)
                sch.wslot("tensor", sl_zl_psum)
                nv = run_inc("tensor", s_pe, lambda Eg, h=h: Eg.matmul(
                    bank(5)[0:1, 0:1], bq_sb[:, h:h + 1], bk_sb[:, h:h + 1],
                    start=True, stop=True))
                sch.wait("vector", s_pe, nv)
                nm_ = run_inc("vector", s_ve, lambda Eg: Eg.tensor_scalar(
                    ncm[0:1, 0:1], bank(5)[0:1, 0:1], -SCALE, -CSHIFT, op0=ALU.mult, op1=ALU.add))
                sch.wait("vector", s_ve, nm_)
                sch.wslot("vector", sl_ncb)
                nb = run_inc("vector", s_ve, lambda Eg: Eg.tensor_copy(ncb[0:1, 0:1], ncm[0:1, 0:1]))
                sl_zl_psum.set(s_ve, nb)
                sch.wait("tensor", s_ve, nb)
                sch.wslot("tensor", sl_rlb_psum)
                bv2 = run_inc("tensor", s_pe, lambda Eg: Eg.matmul(
                    bank(6)[:, 0:1], one1[0:1, :], ncb[0:1, 0:1], start=True, stop=True))
                sl_ncb.set(s_pe, bv2)
                sch.wait("vector", s_pe, bv2)
                nr = run_inc("vector", s_ve, lambda Eg, h=h: Eg.tensor_copy(
                    negc[:, h:h + 1], bank(6)[:, 0:1]))
                sl_rlb_psum.set(s_ve, nr)
            negc_state[0] = s_ve.n

        nchunks = 8 * (p + 1)
        cc_need = 2 * (p + 1)
        for h in range(H):
            sl = h % 2
            sch.wait("sync", s_cc, cc_need)
            sch.wslot("sync", sl_kv[sl])
            if h == 0:
                sch.wait("sync", s_pe, pe_after_p1[min(p + 1, NP - 1)])
            for i in range(p + 1):
                dma("sync", kt_blk[sl][:, 1024 * i:1024 * i + 1024].rearrange("p (r k) -> p r k", r=4),
                    dkta[i][:, h, :, :].transpose([1, 0, 2]), s_kr)
            dma("sync", v_blk[sl][:, 0:128 * nchunks].rearrange("p (s e) -> p s e", e=128),
                dv[0:nchunks, :, DH * h:DH * h + DH].transpose([1, 0, 2]), s_kr)
            kv_r = s_kr.n
            sch.wait("tensor", s_kr, kv_r)
            if h == 0:
                sch.wait("tensor", s_sc, qt_last_evac[p])
                sch.wait("tensor", s_ve, negc_state[0])
                sch.wslot("tensor", sl_kt_psum2[0])
                sch.wslot("tensor", sl_kt_psum2[1])
                sch.wslot("tensor", sl_v_psum2[0])
                sch.wslot("tensor", sl_v_psum2[1])
            for grp in range(NG[p]):
                sb = grp % 2
                sch.wslot("tensor", sl_ss[sb])

                def smm(Eg, sb=sb, grp=grp, sl=sl, h=h, p=p):
                    for c4 in range(4):
                        ch = 4 * grp + c4
                        i = Eg.matmul(bank(2 * sb, 2)[:, 256 * c4:256 * c4 + 256],
                                      kt_blk[sl][:, 128 * ch:128 * ch + 128],
                                      qt2[p % 2][:, 256 * h:256 * h + 256],
                                      start=True, stop=True, skip_group_check=True)
                    return i
                sv_ = run_inc("tensor", s_pe, smm)
                sch.wait("scalar", s_pe, sv_)
                sch.wslot("scalar", sl_pt[grp % 2])
                ex = run_inc("scalar", s_sc, lambda Eg, sb=sb, grp=grp, h=h: Eg.activation(
                    pt[grp % 2][:, :], bank(2 * sb, 2), AF.Exp,
                    bias=negc[:, h:h + 1], scale=SCALE))
                sl_ss[sb].set(s_sc, ex)
                masked = grp >= NG[p] - 2
                if masked:
                    rel = grp - (NG[p] - 2)
                    sch.wait("vector", s_sc, ex)
                    mk = run_inc("vector", s_ve, lambda Eg, grp=grp, rel=rel, p=p: Eg.tensor_mul(
                        pt[grp % 2][:, :], pt[grp % 2][:, :],
                        masks2[p % 2][:, 1024 * rel:1024 * rel + 1024]))
                    sch.wait("tensor", s_ve, mk)
                else:
                    sch.wait("tensor", s_sc, ex)
                if grp == 0:
                    sch.wslot("tensor", sl_zl_psum)

                def avmm(Eg, grp=grp, sl=sl, nchunks=nchunks):
                    for c4 in range(4):
                        ch = 4 * grp + c4
                        first = (ch == 0)
                        last = (ch == nchunks - 1)
                        Eg.matmul(bank(4)[:, 0:256],
                                  v_blk[sl][:, 128 * ch:128 * ch + 128],
                                  pt[grp % 2][:, 256 * c4:256 * c4 + 256],
                                  start=first, stop=last, skip_group_check=True)
                        i = Eg.matmul(bank(5)[:, 0:256], ones_sq[:, :],
                                      pt[grp % 2][:, 256 * c4:256 * c4 + 256],
                                      start=first, stop=last, skip_group_check=True)
                    return i
                av_ = run_inc("tensor", s_pe, avmm)
                sl_pt[grp % 2].set(s_pe, av_)
            sl_kv[sl].set(s_pe)
            sch.wait("vector", s_pe, s_pe.n)
            rb = run_inc("vector", s_ve, lambda Eg: Eg.reciprocal(rlb_sb[:, :], bank(5)[:, 0:256]))
            sch.wait("vector", s_ve, rb)
            ze = run_inc("vector", s_ve, lambda Eg, h=h, p=p: Eg.tensor_tensor(
                zt[:, 1024 * h + 256 * p:1024 * h + 256 * p + 256],
                bank(4)[:, 0:256], rlb_sb[:, :], op=ALU.mult))
            sl_zl_psum.set(s_ve, ze)
        pe_after_p2[p] = s_pe.n
        sc_after_p2[p] = s_sc.n
        st_p2["pe"] = s_pe.n
        st_p2["sc"] = s_sc.n
        st_p2["ve_zt"] = s_ve.n
        sl_qt2[p % 2].set(s_pe)
        sl_masks[p % 2].set(s_ve)

    build_round(0)
    for p in range(rounds):
        if do_p2 and p + 1 < rounds:
            build_round(p + 1)
        elif not do_p2 and p + 1 < rounds:
            build_round(p + 1)
        if do_p2:
            attn_round(p)
        else:
            pe_after_p2[p] = s_pe.n
            sc_after_p2[p] = s_sc.n
    ve_zt_all_read = st_p2["ve_zt"] if do_p2 else s_ve.n

    ln2_done = {}
    pe_ln2 = 0
    kw_xo = 0
    if do_o and do_p2 and rounds == NP:
        sch.wait("gpsimd", s_pe, pe_after_p2[NP - 1])
        for h in range(H):
            dma("gpsimd", wo_bf[:, 2048 * h:2048 * h + 2048], dwo[h][:, :], s_w)
        wo_w = s_w.n
        ve_all_z = s_ve.n
        sl_x_kw = [_Slot(), _Slot()]
        sl_xf.set(s_pe, pe_after_p1[NP - 1])
        for u in range(8):
            pp, kk = u // 2, u % 2
            sl = u % 2
            sch.wslot("sync", sl_x[sl])
            sch.wslot("sync", sl_x_kw[sl])
            xrow = 1024 * pp + 128 * kk
            dma("sync", x_t[sl][:, :], dx[xrow:xrow + 128, :], s_x)
            xv = s_x.n
            if u == 0:
                sch.wait("tensor", s_w, wo_w)
                sch.wait("tensor", s_ve, ve_all_z)
                sch.wait("tensor", s_sc, sc_after_p2[NP - 1])
                sch.wslot("tensor", sl_kt_psum2[0])
                sch.wslot("tensor", sl_kt_psum2[1])
                sch.wslot("tensor", sl_v_psum2[0])
            if u >= 1:
                sch.wait("tensor", s_ve, oproj_ve[u - 1])

            def omm_half(Eg, pp=pp, kk=kk, qs=(0, 1)):
                for h in range(H):
                    zsl = zt[:, 1024 * h + 256 * pp + 128 * kk:1024 * h + 256 * pp + 128 * kk + 128]
                    for q in qs:
                        Eg.matmul(bank(0, 4)[:, 512 * q:512 * q + 512], zsl,
                                  wo_bf[:, 2048 * h + 512 * q:2048 * h + 512 * q + 512],
                                  start=(h == 0), stop=False, skip_group_check=True)
                for q in qs:
                    i = Eg.matmul(bank(0, 4)[:, 512 * q:512 * q + 512], one1[0:1, :],
                                  borow[0:1, 512 * q:512 * q + 512],
                                  start=False, stop=True, skip_group_check=True)
                return i
            ov_lo = run_inc("tensor", s_pe, lambda Eg, f=omm_half: f(Eg, qs=(0, 1)))
            ov_hi = run_inc("tensor", s_pe, lambda Eg, f=omm_half: f(Eg, qs=(2, 3)))
            sch.wait("vector", s_pe, ov_lo)
            sch.wait("vector", s_x, xv)
            oe_lo = run_inc("vector", s_ve, lambda Eg, sl=sl: Eg.tensor_tensor(
                x_t[sl][:, 0:1024], bank(0, 2), x_t[sl][:, 0:1024], op=ALU.add))
            sch.wait("vector", s_pe, ov_hi)
            oe = run_inc("vector", s_ve, lambda Eg, sl=sl: Eg.tensor_tensor(
                x_t[sl][:, 1024:2048], bank(2, 2), x_t[sl][:, 1024:2048], op=ALU.add))
            oproj_ve[u] = oe
            sch.wait("sync", s_ve, oe)
            dma("sync", dxo[128 * u:128 * u + 128, :], x_t[sl][:, :], s_kw)
            sl_x_kw[sl].set(s_kw)
            sch.wait("vector", s_ve, oe)
            s1 = emit_ln(x_t[sl][:, :])
            sl_x[sl].set(s_ve, s1)
            few = (s_ve, ve_zt_all_read) if u == 0 else None
            emit_transposes(128 * u, s1, few)
            ln2_done[u] = s_ve.n
        pe_ln2 = s_pe.n
        kw_xo = s_kw.n

    if do_ffn and do_o and do_p2 and rounds == NP:
        ve_xf2 = s_ve.n

        comb_ve = {}
        rt_prev_sc = [0]
        gw3 = gwT.rearrange("p (c e) -> p c e", c=DC)
        rt_gate = [None]
        for u in range(8):
            sch.wait("tensor", s_ve, ln2_done[u])
            if u == 0:
                sch.wait("tensor", s_w, p0_w)
                sch.wslot("tensor", sl_zl_psum)
                sch.wslot("tensor", sl_rlb_psum)
            else:
                sch.wait("tensor", s_sc, rt_gate[0][0])
                sch.wait("tensor", s_ve, rt_gate[0][1])

            def rmm(Eg, u=u):
                for c in range(DC):
                    Eg.matmul(bank(5)[:, 0:8], xf3[:, c, 128 * u:128 * u + 128],
                              gw3[:, c, :], start=(c == 0), stop=(c == DC - 1),
                              skip_group_check=True)
                    i = Eg.matmul(bank(6)[:, 0:1], xf3[:, c, 128 * u:128 * u + 128],
                                  shgT[:, c:c + 1], start=(c == 0), stop=(c == DC - 1),
                                  skip_group_check=True)
                return i
            rv = run_inc("tensor", s_pe, rmm)
            sch.wait("scalar", s_pe, rv)
            sgv = run_inc("scalar", s_sc, lambda Eg, u=u: Eg.activation(
                sgc[:, u:u + 1], bank(6)[:, 0:1], AF.Sigmoid))
            sgv_gate = sgv
            sch.wait("vector", s_pe, rv)
            sch.wait("vector", s_sc, rt_prev_sc[0])

            def VS(fn):
                v = run_inc("vector", s_ve, fn)
                sch.wait("vector", s_ve, v)
                return v
            cp5 = VS(lambda Eg: Eg.tensor_copy(ln2[:, 0:8], bank(5)[:, 0:8]))
            rt_gate[0] = (sgv_gate, cp5)
            VS(lambda Eg: Eg.reduce_max(ln2[:, 8:9], ln2[:, 0:8], axis=AX.X))
            VS(lambda Eg: Eg.tensor_scalar(ln2[:, 9:10], ln2[:, 8:9], -1.0, None, op0=ALU.mult))
            VS(lambda Eg: Eg.tensor_scalar(ln2[:, 10:18], ln2[:, 0:8], ln2[:, 8:9], None, op0=ALU.is_ge))
            VS(lambda Eg: Eg.tensor_scalar(ln2[:, 18:26], ln2[:, 10:18], -1e9, None, op0=ALU.mult))
            VS(lambda Eg: Eg.tensor_add(ln2[:, 26:34], ln2[:, 18:26], ln2[:, 0:8]))
            VS(lambda Eg: Eg.reduce_max(ln2[:, 34:35], ln2[:, 26:34], axis=AX.X))
            VS(lambda Eg: Eg.tensor_scalar(ln2[:, 35:43], ln2[:, 26:34], ln2[:, 34:35], None, op0=ALU.is_ge))
            kv_ = VS(lambda Eg: Eg.tensor_add(ln2[:, 43:51], ln2[:, 35:43], ln2[:, 10:18]))
            sch.wait("scalar", s_ve, kv_)
            pe_ = run_inc("scalar", s_sc, lambda Eg: Eg.activation(
                ln2[:, 51:59], ln2[:, 0:8], AF.Exp, bias=ln2[:, 9:10], scale=1.0))
            rt_prev_sc[0] = pe_
            sch.wait("vector", s_sc, pe_)
            VS(lambda Eg: Eg.tensor_mul(ln2[:, 18:26], ln2[:, 51:59], ln2[:, 43:51]))
            VS(lambda Eg: Eg.reduce_sum(ln2[:, 59:60], ln2[:, 18:26], axis=AX.X))
            VS(lambda Eg: Eg.reciprocal(ln2[:, 60:61], ln2[:, 59:60]))
            cb = VS(lambda Eg, u=u: Eg.tensor_scalar(
                comb[:, 8 * u:8 * u + 8], ln2[:, 18:26], ln2[:, 60:61], None, op0=ALU.mult))
            comb_ve[u] = cb
            if u == 0:
                sl_zl_psum.set(s_ve, cb)
                sl_rlb_psum.set(s_sc, sgv)
        sch.wait("vector", s_sc, s_sc.n)
        sgv2 = run_inc("vector", s_ve, lambda Eg: Eg.tensor_copy(sgb[:, :], sgc[:, :]))
        comb_ready = sgv2

        xo_r = {}
        sch.wait("sync", s_pe, pe_ln2)
        sch.wait("sync", s_ve, s_ve.n)
        sch.wait("sync", s_kw, kw_xo)
        for u in range(8):
            dma("sync", x_own[u][:, :], dxo[128 * u:128 * u + 128, :], s_x)
            xo_r[u] = s_x.n

        dn_ev = {}
        dn_lo_prev = [0]
        dn_hi_prev = [0]
        ce_done = {}
        h_done = {}
        wgu_ctr = [0]
        wd_ctr = [0]
        first_ffn = [True]

        def phaseA(ee):
            shared = (ee == 8)
            nhc = HSC if shared else HMC
            halves = 2 if shared else 1
            for hf in range(halves):
                if hf == 0:
                    sch.wait("sync", s_ve, comb_ready)
                    sch.wslot("sync", sl_cerow)
                    for uu in range(8):
                        csrc = comb[:, 8 * uu + ee:8 * uu + ee + 1] if ee < 8 else sgb[:, uu:uu + 1]
                        dma("sync", cerow[0:1, 128 * uu:128 * uu + 128], csrc, s_x)
                    cw = s_x.n
                    sch.wait("tensor", s_x, cw)
                    if ee > 0:
                        sch.wait("tensor", s_ve, dn_ev.get((ee - 1, 7), 0) or 0)
                    if first_ffn[0]:
                        sch.wait("tensor", s_ve, oproj_ve[7])
                        first_ffn[0] = False
                    c1 = run_inc("tensor", s_pe, lambda Eg: Eg.matmul(
                        bank(0)[:, 0:512], one1[0:1, :],
                        cerow[0:1, 0:512], start=True, stop=True))
                    c2 = run_inc("tensor", s_pe, lambda Eg: Eg.matmul(
                        bank(1)[:, 0:512], one1[0:1, :],
                        cerow[0:1, 512:1024], start=True, stop=True))
                    sl_cerow.set(s_pe, c2)
                    sch.wait("vector", s_pe, c2)
                    sch.emit("vector", lambda Eg: Eg.tensor_copy(ce_sb[:, 0:512], bank(0)[:, 0:512]))
                    cev = run_inc("vector", s_ve, lambda Eg: Eg.tensor_copy(
                        ce_sb[:, 512:1024], bank(1)[:, 0:512]))
                    ce_done[ee] = cev
                nh = nhc // halves if shared else nhc
                for ih in range(nh):
                    hc = hf * HMC + ih
                    slw = wgu_ctr[0] % 2
                    wgu_ctr[0] += 1
                    if ee == 0 and hf == 0 and ih == 0:
                        sch.wait("gpsimd", s_pe, pe_ln2)
                    sch.wslot("gpsimd", sl_wgu[slw])
                    if shared:
                        srcg = dswg[:, 128 * hc:128 * hc + 128].rearrange("(c p) m -> p c m", p=128)
                        srcu = dswu[:, 128 * hc:128 * hc + 128].rearrange("(c p) m -> p c m", p=128)
                    else:
                        srcg = dwg[ee][:, 128 * hc:128 * hc + 128].rearrange("(c p) m -> p c m", p=128)
                        srcu = dwu[ee][:, 128 * hc:128 * hc + 128].rearrange("(c p) m -> p c m", p=128)
                    dma("gpsimd", wgu_bf[slw][:, 0:DC * 128].rearrange("p (c m) -> p c m", c=DC), srcg, s_w)
                    dma("gpsimd", wgu_bf[slw][:, DC * 128:2 * DC * 128].rearrange("p (c m) -> p c m", c=DC), srcu, s_w)
                    ww = s_w.n
                    sch.wait("tensor", s_w, ww)
                    if ee == 0 and ih == 0:
                        sch.wait("tensor", s_ve, ve_xf2)
                    g2 = h_done.get((ee, hf, ih - 1))
                    if g2:
                        sch.wait("tensor", s_ve, g2)
                    elif hf == 1 and ih == 0:
                        sch.wait("tensor", s_ve, h_done[(ee, 0, nh - 1)])
                    if ih == 0 and hf == 0:
                        sch.wait("tensor", s_ve, ce_done[ee])

                    def gumm(Eg, slw=slw):
                        w3 = wgu_bf[slw].rearrange("p (g c m) -> p g c m", g=2, c=DC)
                        for c in range(DC):
                            for q in range(2):
                                Eg.matmul(bank(0, 2)[:, 512 * q:512 * q + 512], w3[:, 0, c, :],
                                          xf3[:, c, 512 * q:512 * q + 512],
                                          start=(c == 0), stop=(c == DC - 1), skip_group_check=True)
                                i = Eg.matmul(bank(2, 2)[:, 512 * q:512 * q + 512], w3[:, 1, c, :],
                                              xf3[:, c, 512 * q:512 * q + 512],
                                              start=(c == 0), stop=(c == DC - 1), skip_group_check=True)
                        return i
                    gv = run_inc("tensor", s_pe, gumm)
                    sl_wgu[slw].set(s_pe, gv)
                    sch.wait("scalar", s_pe, gv)
                    g3 = h_done.get((ee, hf, ih - 1)) or (h_done.get((ee, 0, nh - 1)) if hf else None)
                    if g3:
                        sch.wait("scalar", s_ve, g3)
                    sgv_ = run_inc("scalar", s_sc, lambda Eg: Eg.activation(
                        sg_sb[:, :], bank(0, 2)[:, 0:1024], AF.Silu))
                    sch.wait("vector", s_pe, gv)
                    sch.wait("vector", s_ve, ce_done[ee])
                    uv_ = run_inc("vector", s_ve, lambda Eg: Eg.tensor_tensor(
                        u_sb[:, :], bank(2, 2)[:, 0:1024], ce_sb[:, :], op=ALU.mult))
                    sch.wait("vector", s_ve, uv_)
                    sch.wait("vector", s_sc, sgv_)
                    if ee == 0 and hf == 0 and ih == 0:
                        sch.wait("vector", s_kw, s_kw.n)
                        sch.wait("vector", s_pe, pe_ln2)
                    hv = run_inc("vector", s_ve, lambda Eg, ih=ih: Eg.tensor_mul(
                        h_sb[:, 1024 * ih:1024 * ih + 1024], sg_sb[:, :], u_sb[:, :]))
                    h_done[(ee, hf, ih)] = hv
                yield hf

        def phaseB(ee, hf):
            shared = (ee == 8)
            nhc = HSC if shared else HMC
            halves = 2 if shared else 1
            nh = nhc // halves if shared else nhc
            pieces = [(0, list(range(0, min(6, nh)))), (1, list(range(6, nh)))]
            for slw, chs in pieces:
                if not chs:
                    continue
                if ee == 0 and hf == 0:
                    sch.wait("gpsimd", s_pe, pe_ln2)
                sch.wslot("gpsimd", sl_wd[slw])
                for k, ihc in enumerate(chs):
                    hc = hf * HMC + ihc
                    srcd = dswd[128 * hc:128 * hc + 128, :] if shared else dwd[ee][128 * hc:128 * hc + 128, :]
                    dma("gpsimd", wd_bf[slw][:, 2048 * k:2048 * k + 2048], srcd, s_w)
            ww = s_w.n
            sch.wait("tensor", s_w, ww)
            sch.wait("tensor", s_ve, h_done[(ee, hf, nh - 1)])
            for u in range(8):
                if dn_lo_prev[0]:
                    sch.wait("tensor", s_ve, dn_lo_prev[0])
                elif ee == 0 and u == 0:
                    sch.wait("tensor", s_ve, s_ve.n)
                    sch.wait("tensor", s_sc, s_sc.n)

                def dmm_half(Eg, u=u, nh=nh, qs=(0, 1)):
                    for ihc in range(nh):
                        slw, k = (0, ihc) if ihc < 6 else (1, ihc - 6)
                        for q in qs:
                            i = Eg.matmul(bank(4, 4)[:, 512 * q:512 * q + 512],
                                          h_sb[:, 1024 * ihc + 128 * u:1024 * ihc + 128 * u + 128],
                                          wd_bf[slw][:, 2048 * k + 512 * q:2048 * k + 512 * q + 512],
                                          start=(ihc == 0), stop=(ihc == nh - 1),
                                          skip_group_check=True)
                    return i
                dv_lo = run_inc("tensor", s_pe, lambda Eg, f=dmm_half: f(Eg, qs=(0, 1)))
                if dn_hi_prev[0]:
                    sch.wait("tensor", s_ve, dn_hi_prev[0])
                dv_ = run_inc("tensor", s_pe, lambda Eg, f=dmm_half: f(Eg, qs=(2, 3)))
                sch.wait("vector", s_pe, dv_lo)
                if ee == 0 and hf == 0:
                    sch.wait("vector", s_x, xo_r[u])
                de_lo = run_inc("vector", s_ve, lambda Eg, u=u: Eg.tensor_tensor(
                    x_own[u][:, 0:1024], x_own[u][:, 0:1024], bank(4, 2), op=ALU.add))
                dn_lo_prev[0] = de_lo
                sch.wait("vector", s_pe, dv_)
                de = run_inc("vector", s_ve, lambda Eg, u=u: Eg.tensor_tensor(
                    x_own[u][:, 1024:2048], x_own[u][:, 1024:2048], bank(6, 2), op=ALU.add))
                dn_hi_prev[0] = de
                dn_ev[(ee, u if hf == 0 else 8 + u)] = de
            for slw, chs in pieces:
                if chs:
                    sl_wd[slw].set(s_pe)

        for ee in range(experts):
            for hf in phaseA(ee):
                phaseB(ee, hf)

        last = experts - 1
        for u in range(8):
            key = (last, 8 + u) if last == 8 else (last, u)
            sch.wait("sync", s_ve, dn_ev[key])
            dma("sync", dout[128 * u:128 * u + 128, :], x_own[u][:, :], s_out)
        sch.wait("sync", s_out, s_out.n)

    if taps:
        tsrc = {"tap_xf": xf, "tap_qt": qt, "tap_zt": zt, "tap_comb": comb, "tap_sg": sgc}
        for t in taps:
            if t == "tap_v":
                sch.wait("sync", s_kw, s_kw.n)
                dma("sync", tap_aps[t][:, :, :], dv[:, :, :], s_out)
            elif t == "tap_xo":
                sch.wait("sync", s_kw, s_kw.n)
                dma("sync", tap_aps[t][:, :], dxo[:, :], s_out)
            elif t == "tap_xown":
                sch.wait("sync", s_ve, s_ve.n)
                for u in range(8):
                    dma("sync", tap_aps[t][128 * u:128 * u + 128, :], x_own[u][:, :], s_out)
            else:
                sch.wait("sync", s_ve, s_ve.n)
                sch.wait("sync", s_sc, s_sc.n)
                dma("sync", tap_aps[t][:, :], tsrc[t][:, :], s_out)
        sch.wait("sync", s_out, s_out.n)

    sch.check_deadlock()
    with nc.Block() as block:
        @block.sync
        def _(Eg):
            for f in sch.q["sync"]:
                f(Eg)

        @block.gpsimd
        def _(Eg):
            for f in sch.q["gpsimd"]:
                f(Eg)

        @block.tensor
        def _(Eg):
            for f in sch.q["tensor"]:
                f(Eg)

        @block.vector
        def _(Eg):
            for f in sch.q["vector"]:
                f(Eg)

        @block.scalar
        def _(Eg):
            for f in sch.q["scalar"]:
                f(Eg)

    st.close()
    return nc


def _host_inputs(inputs, core):
    b, j = core // 4, core % 4
    perm = _perm_tokens(j)
    x = np.ascontiguousarray(np.asarray(inputs["L0_attn_x"])[b][perm])
    return {
        "x": x,
        "wq": np.asarray(inputs["L0_attn_W_Q"]),
        "wk": np.asarray(inputs["L0_attn_W_K"]),
        "wv": np.asarray(inputs["L0_attn_W_V"]),
        "wo": np.asarray(inputs["L0_attn_W_O"]),
        "bq": np.ascontiguousarray(np.asarray(inputs["L0_attn_b_Q"]).T),
        "bk": np.ascontiguousarray(np.asarray(inputs["L0_attn_b_K"]).T),
        "bv": np.asarray(inputs["L0_attn_b_V"]),
        "bo": np.asarray(inputs["L0_attn_b_O"]),
        "gw": np.ascontiguousarray(np.asarray(inputs["L0_ffn_gate_w"]).T),
        "wg": np.asarray(inputs["L0_ffn_w_gate"]),
        "wu": np.asarray(inputs["L0_ffn_w_up"]),
        "wd": np.asarray(inputs["L0_ffn_w_down"]),
        "shg": np.ascontiguousarray(np.asarray(inputs["L0_ffn_shgate_w"]).reshape(DC, 128).T),
        "swg": np.asarray(inputs["L0_ffn_shw_gate"]),
        "swu": np.asarray(inputs["L0_ffn_shw_up"]),
        "swd": np.asarray(inputs["L0_ffn_shw_down"]),
        "masks": _masks(j),
        "iden": np.eye(128, dtype=np.float32),
    }


def kernel(**inputs):
    nc = build_program()
    in_maps = [_host_inputs(inputs, c) for c in range(NCORES)]
    res = run_bass_kernel_spmd(nc, in_maps, core_ids=list(range(NCORES)))
    out = np.zeros((B, S, D), dtype=np.float32)
    for c in range(NCORES):
        b, j = c // 4, c % 4
        o = res.results[c]["out"]
        for i, blk in enumerate(_own_blocks(j)):
            out[b, 256 * blk:256 * blk + 256, :] = o[256 * i:256 * i + 256, :]
    return out
